# version 04
# baseline (speedup 1.0000x reference)
import math
from contextlib import ExitStack

import numpy as np
import ml_dtypes

import concourse.bass as bass
import concourse.mybir as mybir
from concourse.bass_utils import run_bass_kernel_spmd
from concourse.bass_isa import ReduceOp
from concourse.library_config import all_libraries

F32 = mybir.dt.float32
BF16 = mybir.dt.bfloat16
F16 = mybir.dt.float16
I32 = mybir.dt.int32
AF = mybir.ActivationFunctionType
ALU = mybir.AluOpType
AX = mybir.AxisListType.X
LIB_ATTN = [l for l in all_libraries if l.name == "attn"][0]

T = 2048
D = 2048
H = 16
KVH = 4
DH = 128
WINDOW = 1024
NM = 16
QM = 4
NQ = 4
IO = 11264
I5 = 5632
NIT = 44
GUT = 88
GUT_SL = 11
DN_ROWS = 256
NCORE = 8
FB = 512
NFB = T // FB
EPS_LN = 1e-5
EPS_RMS = 1e-6
NEG = -1e30


class Buf:
    __slots__ = ("name", "excl", "w", "readers")

    def __init__(self, name, excl=False):
        self.name = name
        self.excl = excl
        self.w = None
        self.readers = []


class Prog:
    ENGS = ("pe", "dve", "act", "sp", "gp")

    def __init__(self):
        self.ops = []
        self.barrier_deps = []

    SEQ = False
    ALLDRAIN = False
    SEQTAG = ""
    tag = "other"
    SEQA = False
    SEQF = False
    in_ffn = False
    PEDRAIN = False

    def emit(self, eng, fn, reads=(), writes=(), kind="c"):
        i = len(self.ops)
        deps = set(self.barrier_deps)
        if i > 0 and (self.SEQ or (self.SEQA and not self.in_ffn) or (self.SEQF and self.in_ffn)
                      or (self.SEQTAG and self.tag in self.SEQTAG.split(","))):
            deps.add(i - 1)
        for b in reads:
            if b.excl:
                if b.w is not None:
                    deps.add(b.w)
                deps.update(b.readers)
                b.w = i
                b.readers = []
            else:
                if b.w is not None:
                    deps.add(b.w)
                b.readers.append(i)
        for b in writes:
            if b.w is not None:
                deps.add(b.w)
            deps.update(b.readers)
            b.w = i
            b.readers = []
        deps.discard(i)
        self.ops.append(dict(eng=eng, fn=fn, deps=deps, kind=kind, inc=False))
        return i

    def barrier(self):
        last = {}
        for i, op in enumerate(self.ops):
            last[op["eng"]] = i
        self.barrier_deps = list(last.values())

    def resolve_and_replay(self, nc, engines, sems, dma_sems, cc_sem):
        ops = self.ops
        for i, op in enumerate(ops):
            for d in op["deps"]:
                p = ops[d]
                if p["kind"] == "dma" or p["kind"] == "coll":
                    pass
                elif p["eng"] == op["eng"]:
                    pass
                else:
                    p["inc"] = True
        cnt = {e: 0 for e in self.ENGS}
        dma_cnt = {e: 0 for e in self.ENGS}
        cc_cnt = 0
        for op in ops:
            if op["kind"] == "dma":
                dma_cnt[op["eng"]] += 1
                op["done"] = ("dma", op["eng"], 16 * dma_cnt[op["eng"]])
            elif op["kind"] == "coll":
                cc_cnt += 1
                op["done"] = ("cc", None, cc_cnt)
            else:
                if op["inc"]:
                    cnt[op["eng"]] += 2 if (self.ALLDRAIN or (op["eng"] == "pe" and self.PEDRAIN)) else 1
                op["done"] = ("c", op["eng"], cnt[op["eng"]])
        self.total_dma = dict(dma_cnt)
        self.total_cc = cc_cnt
        streams = {e: [] for e in self.ENGS}
        for i, op in enumerate(ops):
            streams[op["eng"]].append(i)

        def run_stream(eng_name, eng):
            watermark = {}
            need_drain_upto = -1
            last_drain = -1
            my_pos = {}
            for pos, i in enumerate(streams[eng_name]):
                my_pos[i] = pos
            for i in streams[eng_name]:
                op = ops[i]
                for d in sorted(op["deps"]):
                    p = ops[d]
                    if p["kind"] == "dma":
                        key = ("dma", p["eng"])
                        val = p["done"][2]
                        if p["eng"] == eng_name:
                            pass
                        if watermark.get(key, 0) < val:
                            eng.wait_ge(dma_sems[p["eng"]], val)
                            watermark[key] = val
                    elif p["kind"] == "coll":
                        key = ("cc",)
                        val = p["done"][2]
                        if watermark.get(key, 0) < val:
                            eng.wait_ge(cc_sem, val)
                            watermark[key] = val
                    elif p["eng"] == eng_name:
                        if my_pos[d] > last_drain:
                            need_drain_upto = max(need_drain_upto, my_pos[d])
                    else:
                        key = ("c", p["eng"])
                        val = p["done"][2]
                        if val > 0 and watermark.get(key, 0) < val:
                            eng.wait_ge(sems[p["eng"]], val)
                            watermark[key] = val
                if need_drain_upto >= 0 and eng_name != "pe":
                    eng.drain()
                    last_drain = my_pos[i] - 1
                    need_drain_upto = -1
                inst = op["fn"](eng)
                if op["kind"] == "dma":
                    inst.then_inc(dma_sems[eng_name], 16)
                elif op["kind"] == "coll":
                    inst.then_inc(cc_sem, 1)
                elif op["inc"]:
                    inst.then_inc(sems[eng_name], 1)
                    if self.ALLDRAIN or (eng_name == "pe" and self.PEDRAIN):
                        eng.drain().then_inc(sems[eng_name], 1)
            return None

        return run_stream


def ceil_div(a, b):
    return (a + b - 1) // b


def build_nc(debug=False):
    nc = bass.Bass("TRN2", target_bir_lowering=False)

    d_x = nc.dram_tensor("x", [T, D], F32, kind="ExternalInput").ap()
    d_wq = nc.dram_tensor("wq", [D, D], F32, kind="ExternalInput").ap()
    d_wk = nc.dram_tensor("wk", [D, 512], F32, kind="ExternalInput").ap()
    d_wv = nc.dram_tensor("wv", [D, 512], F32, kind="ExternalInput").ap()
    d_wg = nc.dram_tensor("wg", [D, D], F32, kind="ExternalInput").ap()
    d_gup = nc.dram_tensor("gup_packed_sl", [1408, 256], I32, kind="ExternalInput").ap()
    d_gus = nc.dram_tensor("gu_scales_sl", [1408, 16], F16, kind="ExternalInput").ap()
    d_dnp = nc.dram_tensor("dn_packed_sl", [DN_ROWS, 704], I32, kind="ExternalInput").ap()
    d_dns = nc.dram_tensor("dn_scales_sl", [DN_ROWS, NIT], F16, kind="ExternalInput").ap()
    d_scaler = nc.dram_tensor("scaler_b", [128, H], F32, kind="ExternalInput").ap()
    d_gnw = nc.dram_tensor("gnw_b", [128, 128], F32, kind="ExternalInput").ap()
    d_cos = nc.dram_tensor("cosb", [128, NM * 64], F32, kind="ExternalInput").ap()
    d_sin = nc.dram_tensor("sinb", [128, NM * 64], F32, kind="ExternalInput").ap()
    d_logp = nc.dram_tensor("logpos", [128, NM], F32, kind="ExternalInput").ap()
    d_ident = nc.dram_tensor("ident", [128, 128], BF16, kind="ExternalInput").ap()
    d_identf = nc.dram_tensor("identf", [128, 128], F32, kind="ExternalInput").ap()
    d_onesr = nc.dram_tensor("onesr", [1, 128], F32, kind="ExternalInput").ap()
    d_mTL = nc.dram_tensor("mTL", [128, 128], F32, kind="ExternalInput").ap()
    d_mTR = nc.dram_tensor("mTR", [128, 128], F32, kind="ExternalInput").ap()
    d_out = nc.dram_tensor("out", [T, D], F32, kind="ExternalOutput").ap()
    d_wbf = nc.dram_tensor("wbf", [D, 5120], BF16).ap()
    d_x1 = nc.dram_tensor("x1sc", [T, D], F32).ap()
    d_agin_gu = nc.dram_tensor("agin_gu", [GUT_SL, 128, 2048], BF16).ap()
    d_agout_gu = nc.dram_tensor("agout_gu", [GUT, 128, 2048], BF16, addr_space="Shared").ap()
    d_agin_dn = nc.dram_tensor("agin_dn", [2, 128, NIT, 128], BF16).ap()
    if debug:
        d_dbg_x1 = nc.dram_tensor("dbg_x1", [T, D], F32, kind="ExternalOutput").ap()
        d_dbg_gu = nc.dram_tensor("dbg_gu", [GUT_SL, 128, 2048], BF16, kind="ExternalOutput").ap()
        d_dbg_dn = nc.dram_tensor("dbg_dn", [2, 128, NIT, 128], BF16, kind="ExternalOutput").ap()
        d_dbg_wbf = nc.dram_tensor("dbg_wbf", [D, 5120], BF16, kind="ExternalOutput").ap()
        d_dbg_act = nc.dram_tensor("dbg_act", [10, 128, 8320], BF16, kind="ExternalOutput").ap()
    d_agout_dn = nc.dram_tensor("agout_dn", [NCORE, 2, 128, NIT, 128], BF16, addr_space="Shared").ap()

    P = Prog()
    import os as _os
    P.SEQ = bool(_os.environ.get("BASSSEQ"))
    P.PEDRAIN = bool(_os.environ.get("BASSPD", "1")) and not _os.environ.get("BASSNOPD")
    P.SEQA = bool(_os.environ.get("BASSSEQA"))
    P.SEQF = bool(_os.environ.get("BASSSEQF"))
    P.SEQTAG = _os.environ.get("BASSSEQT", "p1,p1c,p1t")
    P.ALLDRAIN = bool(_os.environ.get("BASSAD"))
    es = ExitStack()

    def sb(name, shape, dt):
        return es.enter_context(nc.sbuf_tensor(name, shape, dt))

    s_consts_cos = sb("c_cos", [128, NM * 64], F32)
    s_consts_sin = sb("c_sin", [128, NM * 64], F32)
    s_logp = sb("c_logp", [128, NM], F32)
    s_scaler = sb("c_scaler", [128, H], F32)
    s_gnw = sb("c_gnw", [128, 128], F32)
    s_ident = sb("c_ident", [128, 128], BF16)
    s_identf = sb("c_identf", [128, 128], F32)
    s_ones1 = sb("c_ones1", [1, 128], F32)
    s_mTL = sb("c_mTL", [128, 128], F32)
    s_mTR = sb("c_mTR", [128, 128], F32)
    s_cb = sb("c_cb", [128, 4], F32)

    s_lnT = sb("lnT", [128, 16 * 512], BF16)
    s_qT = sb("qT", [128, H * 512], BF16)
    s_kT = sb("kT", [128, KVH * T], BF16)
    s_vaug = sb("vaug", [128, NM * 520], BF16)
    s_sg = sb("sg", [128, QM * D], BF16)
    s_xt = [sb(f"xt{i}", [128, D], F32) for i in range(2)]
    s_wt = [sb(f"wt{i}", [128, 512], BF16) for i in range(4)]
    s_qraw = [sb(f"qraw{i}", [128, 512], F32) for i in range(2)]
    s_rope = sb("rope", [128, 1024], F32)
    s_ropeo = sb("ropeo", [128, 512], F32)
    s_qpost = sb("qpost", [128, 512], BF16)
    s_ms = sb("ms", [128, 64], F32)
    s_r4 = sb("r4", [128, 16], F32)
    s_stat = [sb(f"stat{i}", [128, 8], F32) for i in range(2)]
    s_pT = [sb(f"pT{i}", [128, 9 * 128], BF16) for i in range(2)]
    s_ym = [sb(f"ym{i}", [128, D], F32) for i in range(2)]
    s_x1st = [sb(f"x1st{i}", [128, D], F32) for i in range(2)]
    s_dump = sb("dump", [128, D], BF16)
    s_lnst = sb("lnst", [128, 16], F32)
    s_cast32 = [sb(f"cast32_{i}", [128, 512], F32) for i in range(2)]
    s_castbf = [sb(f"castbf{i}", [128, 512], BF16) for i in range(2)]
    s_dqp = [sb(f"dqp{i}", [128, 704], I32) for i in range(1)]
    s_dqs = sb("dqs", [128, NIT], F32)
    s_dqs16 = sb("dqs16", [128, NIT], F16)
    s_dqv = sb("dqv", [128, 1408], F32)
    s_dqt = [sb(f"dqt{i}", [128, 256], I32) for i in range(3)]
    s_dqf = [sb(f"dqf{i}", [128, 256], F32) for i in range(2)]
    s_dqb = sb("dqb", [128, 2048], BF16)
    s_stg = sb("stg", [128, 2048], BF16)
    s_stgd = sb("stgd", [128, 1408], BF16)

    ps = es.enter_context(nc.psum_tensor("ps", [128, 4096], F32))

    b_consts = Buf("consts")
    b_wbf = {w: Buf(f"wbf_{w}") for w in "qkvg"}
    b_x1hbm = Buf("x1hbm")
    b_agin_gu = Buf("agin_gu")
    b_agin_dn = Buf("agin_dn")
    b_agout = Buf("agout")
    b_lnT = Buf("lnT")
    b_qT = Buf("qT")
    b_kT = Buf("kT")
    b_vaug = Buf("vaug")
    b_sg = Buf("sg")
    b_xt = [Buf(f"xt{i}") for i in range(2)]
    b_xe = b_xt
    b_wt = [Buf(f"wt{i}") for i in range(4)]
    b_qraw = [Buf(f"qraw{i}") for i in range(2)]
    b_rope = Buf("rope")
    b_ropeo = Buf("ropeo")
    b_qpost = Buf("qpost")
    b_ms = Buf("ms")
    b_r4 = Buf("r4")
    b_stat = [Buf(f"stat{i}") for i in range(2)]
    b_pT = [Buf(f"pT{i}") for i in range(2)]
    b_ym = [Buf(f"ym{i}") for i in range(2)]
    b_x1st = [Buf(f"x1st{i}") for i in range(2)]
    b_dump = Buf("dump")
    b_lnst = Buf("lnst")
    b_cast32 = [Buf(f"cast32_{i}") for i in range(2)]
    b_castbf = [Buf(f"castbf{i}") for i in range(2)]
    b_dq = Buf("dq")
    b_stg = Buf("stg")
    b_stgd = Buf("stgd")
    b_bank = [Buf(f"bank{i}", excl=True) for i in range(8)]
    b_outhbm = Buf("outhbm")

    def banks(lo, hi):
        return [b_bank[i] for i in range(lo, hi + 1)]

    E = P.emit

    def ld(dst, src):
        E("gp", lambda g, dst=dst, src=src: g.dma_start(out=dst, in_=src),
          writes=[b_consts], kind="dma")

    ld(s_consts_cos[:], d_cos)
    ld(s_consts_sin[:], d_sin)
    ld(s_logp[:], d_logp)
    ld(s_scaler[:], d_scaler)
    ld(s_gnw[:], d_gnw)
    ld(s_ident[:], d_ident)
    ld(s_identf[:], d_identf)
    ld(s_ones1[:], d_onesr)
    ld(s_mTL[:], d_mTL)
    ld(s_mTR[:], d_mTR)
    E("dve", lambda v: v.memset(s_cb[:, 0:1], -0.5), writes=[b_consts])
    E("dve", lambda v: v.memset(s_vaug[:], 1.0), writes=[b_vaug])

    wsrc = [("q", d_wq, 2048, 0), ("k", d_wk, 512, 2048), ("v", d_wv, 512, 2560), ("g", d_wg, 2048, 3072)]
    slot_c = 0
    for wname, dsrc, wcols, coff in wsrc:
        for dc in range(16):
            for cc in range(0, wcols, 512):
                sl = slot_c % 2
                slot_c += 1
                E("sp", lambda s, dsrc=dsrc, dc=dc, sl=sl, cc=cc:
                  s.dma_start(out=s_cast32[sl][:], in_=dsrc[dc * 128:(dc + 1) * 128, cc:cc + 512]),
                  writes=[b_cast32[sl]], kind="dma")
                E("dve", lambda v, sl=sl: v.tensor_copy(out=s_castbf[sl][:], in_=s_cast32[sl][:]),
                  reads=[b_cast32[sl]], writes=[b_castbf[sl]])
                E("gp", lambda g, dc=dc, sl=sl, cc=cc, coff=coff:
                  g.dma_start(out=d_wbf[dc * 128:(dc + 1) * 128, coff + cc:coff + cc + 512], in_=s_castbf[sl][:]),
                  reads=[b_castbf[sl]], writes=[b_wbf[wname]], kind="dma")

    def emit_ln_quarter(q, src_hbm, b_srchbm, eps):
        for mloc in range(QM):
            m = q * QM + mloc
            sl = mloc % 2
            E("sp", lambda s, m=m, sl=sl, src_hbm=src_hbm:
              s.dma_start(out=s_xt[sl][:], in_=src_hbm[m * 128:(m + 1) * 128, :]),
              reads=([b_srchbm] if b_srchbm else []), writes=[b_xt[sl]], kind="dma")
            E("dve", lambda v, sl=sl, mloc=mloc:
              v.tensor_reduce(out=s_lnst[:, mloc:mloc + 1], in_=s_xt[sl][:], axis=AX, op=ALU.add),
              reads=[b_xt[sl]], writes=[b_lnst])
            E("act", lambda a, sl=sl, mloc=mloc:
              a.activation(out=s_dump[:], in_=s_xt[sl][:], func=AF.Square,
                           accum_out=s_lnst[:, 4 + mloc:5 + mloc]),
              reads=[b_xt[sl]], writes=[b_dump, b_lnst])
            E("dve", lambda v, mloc=mloc:
              v.tensor_scalar(out=s_lnst[:, 8 + mloc:9 + mloc], in0=s_lnst[:, mloc:mloc + 1],
                              scalar1=1.0 / D, scalar2=None, op0=ALU.mult),
              reads=[b_lnst], writes=[b_lnst])
            E("dve", lambda v, mloc=mloc:
              v.scalar_tensor_tensor(out=s_lnst[:, 12 + mloc:13 + mloc], in0=s_lnst[:, 8 + mloc:9 + mloc],
                                     scalar=-1.0, in1=s_lnst[:, 8 + mloc:9 + mloc], op0=ALU.mult, op1=ALU.mult),
              reads=[b_lnst], writes=[b_lnst])
            E("dve", lambda v, mloc=mloc, eps=eps:
              v.scalar_tensor_tensor(out=s_lnst[:, 12 + mloc:13 + mloc], in0=s_lnst[:, 4 + mloc:5 + mloc],
                                     scalar=1.0 / D, in1=s_lnst[:, 12 + mloc:13 + mloc], op0=ALU.mult, op1=ALU.add),
              reads=[b_lnst], writes=[b_lnst])
            E("dve", lambda v, mloc=mloc, eps=eps:
              v.tensor_scalar(out=s_lnst[:, 12 + mloc:13 + mloc], in0=s_lnst[:, 12 + mloc:13 + mloc],
                              scalar1=1.0, scalar2=eps, op0=ALU.mult, op1=ALU.add),
              reads=[b_lnst], writes=[b_lnst])
            E("act", lambda a, mloc=mloc: a.activation(out=s_lnst[:, 12 + mloc:13 + mloc], in_=s_lnst[:, 12 + mloc:13 + mloc], func=AF.Ln),
              reads=[b_lnst], writes=[b_lnst])
            E("act", lambda a, mloc=mloc: a.activation(out=s_lnst[:, 12 + mloc:13 + mloc], in_=s_lnst[:, 12 + mloc:13 + mloc], func=AF.Exp,
                                        scale=s_cb[:, 0:1]),
              reads=[b_lnst, b_consts], writes=[b_lnst])
            E("dve", lambda v, mloc=mloc:
              v.scalar_tensor_tensor(out=s_lnst[:, 8 + mloc:9 + mloc], in0=s_lnst[:, 8 + mloc:9 + mloc],
                                     scalar=-1.0, in1=s_lnst[:, 12 + mloc:13 + mloc], op0=ALU.mult, op1=ALU.mult),
              reads=[b_lnst], writes=[b_lnst])
            E("act", lambda a, sl=sl, mloc=mloc:
              a.activation(out=s_dump[:], in_=s_xt[sl][:], func=AF.Identity,
                           bias=s_lnst[:, 8 + mloc:9 + mloc], scale=s_lnst[:, 12 + mloc:13 + mloc]),
              reads=[b_xt[sl], b_lnst], writes=[b_dump])
            pbf = ps[:, 0:1024].bitcast(BF16)
            for dc in range(16):
                E("pe", lambda t, dc=dc, pbf=pbf:
                  t.transpose(pbf[:, dc * 128:(dc + 1) * 128], s_dump[:, dc * 128:(dc + 1) * 128], s_ident[:]),
                  reads=[b_dump, b_consts] + banks(0, 1), writes=[])
            dst = s_lnT[:, :].rearrange("p (dc t) -> p dc t", dc=16)[:, :, mloc * 128:(mloc + 1) * 128]
            E("act", lambda a, dst=dst, pbf=pbf:
              a.activation(out=dst, in_=pbf.rearrange("p (dc t) -> p dc t", dc=16), func=AF.Copy),
              reads=banks(0, 1), writes=[b_lnT])

    def emit_proj_quarter(q):
        jobs = []
        for oc in range(4):
            jobs.append(("q", oc, oc * 512))
        jobs.append(("k", 0, 2048))
        jobs.append(("v", 0, 2560))
        for oc in range(4):
            jobs.append(("g", oc, 3072 + oc * 512))
        slot = 0
        for wname, oc, coff in jobs:
            for dc in range(16):
                sl = slot % 4
                slot += 1
                E("sp", lambda s, dc=dc, coff=coff, sl=sl:
                  s.dma_start(out=s_wt[sl][:], in_=d_wbf[dc * 128:(dc + 1) * 128, coff:coff + 512]),
                  reads=[b_wbf[wname]], writes=[b_wt[sl]], kind="dma")
                for mloc in range(QM):
                    lhsT = s_lnT[:, dc * 512 + mloc * 128: dc * 512 + (mloc + 1) * 128]
                    pm = ps[:, 1024 + mloc * 512: 1024 + (mloc + 1) * 512]
                    E("pe", lambda t, lhsT=lhsT, pm=pm, sl=sl, dc=dc:
                      t.matmul(pm, lhsT, s_wt[sl][:], start=(dc == 0), stop=(dc == 15),
                               skip_group_check=True),
                      reads=[b_lnT, b_wt[sl]] + [b_bank[2 + mloc]], writes=[])
            for mloc in range(QM):
                m = q * QM + mloc
                pm = ps[:, 1024 + mloc * 512: 1024 + (mloc + 1) * 512]
                if wname == "q" or wname == "k":
                    emit_qk_post(q, m, mloc, oc, wname, pm)
                elif wname == "v":
                    dst = s_vaug[:, m * 520: m * 520 + 520].rearrange("p (kv c) -> p kv c", kv=4)[:, :, 0:128]
                    E("act", lambda a, dst=dst, pm=pm:
                      a.activation(out=dst, in_=pm.rearrange("p (kv c) -> p kv c", kv=4), func=AF.Copy),
                      reads=[b_bank[2 + mloc]], writes=[b_vaug])
                else:
                    E("act", lambda a, pm=pm, mloc=mloc, oc=oc:
                      a.activation(out=s_sg[:, mloc * D + oc * 512: mloc * D + (oc + 1) * 512],
                                   in_=pm, func=AF.Silu),
                      reads=[b_bank[2 + mloc]], writes=[b_sg])

    def emit_qk_post(q, m, mloc, oc, wname, pm):
        _oldtag = P.tag
        P.tag = "p1c"
        sl = m % 2
        E("act", lambda a, pm=pm, sl=sl: a.activation(out=s_qraw[sl][:], in_=pm, func=AF.Copy),
          reads=[pm_bank(mloc)], writes=[b_qraw[sl]])
        for hh in range(4):
            E("dve", lambda v, sl=sl, hh=hh:
              v.scalar_tensor_tensor(out=s_dump[:].bitcast(F32)[:, 0:128], in0=s_qraw[sl][:, hh * 128:(hh + 1) * 128],
                                     scalar=1.0, in1=s_qraw[sl][:, hh * 128:(hh + 1) * 128],
                                     op0=ALU.mult, op1=ALU.mult, accum_out=s_ms[:, hh:hh + 1]),
              reads=[b_qraw[sl]], writes=[b_dump, b_ms])
        E("dve", lambda v: v.tensor_scalar(out=s_ms[:, 0:4], in0=s_ms[:, 0:4], scalar1=1.0 / DH,
                                           scalar2=EPS_RMS, op0=ALU.mult, op1=ALU.add),
          reads=[b_ms], writes=[b_ms])
        E("act", lambda a: a.activation(out=s_r4[:, 0:4], in_=s_ms[:, 0:4], func=AF.Ln),
          reads=[b_ms], writes=[b_r4])
        E("act", lambda a: a.activation(out=s_r4[:, 0:4], in_=s_r4[:, 0:4], func=AF.Exp, scale=s_cb[:, 0:1]),
          reads=[b_r4, b_consts], writes=[b_r4])
        if wname == "q":
            E("dve", lambda v, oc=oc:
              v.tensor_tensor(out=s_r4[:, 4:8], in0=s_r4[:, 0:4], in1=s_scaler[:, oc * 4:(oc + 1) * 4], op=ALU.mult),
              reads=[b_r4, b_consts], writes=[b_r4])
            E("dve", lambda v, m=m:
              v.tensor_scalar(out=s_r4[:, 4:8], in0=s_r4[:, 4:8], scalar1=s_logp[:, m:m + 1],
                              scalar2=1.0 / math.sqrt(DH), op0=ALU.mult, op1=ALU.mult),
              reads=[b_r4, b_consts], writes=[b_r4])
            rcol = 4
        else:
            rcol = 0
        qv = s_qraw[sl][:, 0:512].rearrange("p (h twod) -> p h twod", h=4)
        ov = s_ropeo[:, 0:512].rearrange("p (h twod) -> p h twod", h=4)
        X1, X2 = qv[:, :, 0:64], qv[:, :, 64:128]
        Y1, Y2 = ov[:, :, 0:64], ov[:, :, 64:128]
        C = s_consts_cos[:, m * 64:(m + 1) * 64].rearrange("p (one d) -> p one d", one=1).broadcast_to([128, 4, 64])
        S = s_consts_sin[:, m * 64:(m + 1) * 64].rearrange("p (one d) -> p one d", one=1).broadcast_to([128, 4, 64])
        rA = s_rope[:, 0:256].rearrange("p (h d) -> p h d", h=4)
        rB = s_rope[:, 256:512].rearrange("p (h d) -> p h d", h=4)
        rD = s_rope[:, 512:768].rearrange("p (h d) -> p h d", h=4)
        rE = s_rope[:, 768:1024].rearrange("p (h d) -> p h d", h=4)
        E("dve", lambda v, X1=X1, C=C, rA=rA: v.tensor_tensor(out=rA, in0=X1, in1=C, op=ALU.mult),
          reads=[b_qraw[sl], b_consts], writes=[b_rope])
        E("dve", lambda v, X2=X2, S=S, rB=rB: v.tensor_tensor(out=rB, in0=X2, in1=S, op=ALU.mult),
          reads=[b_qraw[sl], b_consts], writes=[b_rope])
        E("dve", lambda v, X1=X1, S=S, rD=rD: v.tensor_tensor(out=rD, in0=X1, in1=S, op=ALU.mult),
          reads=[b_qraw[sl], b_consts], writes=[b_rope])
        E("dve", lambda v, X2=X2, C=C, rE=rE: v.tensor_tensor(out=rE, in0=X2, in1=C, op=ALU.mult),
          reads=[b_qraw[sl], b_consts], writes=[b_rope])
        E("dve", lambda v, Y1=Y1, rA=rA, rB=rB: v.tensor_tensor(out=Y1, in0=rA, in1=rB, op=ALU.add),
          reads=[b_rope], writes=[b_ropeo])
        E("dve", lambda v, Y2=Y2, rE=rE, rD=rD: v.tensor_tensor(out=Y2, in0=rE, in1=rD, op=ALU.subtract),
          reads=[b_rope], writes=[b_ropeo])
        for hh in range(4):
            E("dve", lambda v, hh=hh, rcol=rcol:
              v.tensor_scalar(out=s_qpost[:, hh * 128:(hh + 1) * 128], in0=s_ropeo[:, hh * 128:(hh + 1) * 128],
                              scalar1=s_r4[:, rcol + hh:rcol + hh + 1], scalar2=None, op0=ALU.mult),
              reads=[b_ropeo, b_r4], writes=[b_qpost])
        P.tag = "p1t"
        pbf6 = ps[:, 3072:3584].bitcast(BF16)
        for hh in range(4):
            slot6 = hh % 8
            E("pe", lambda t, hh=hh, slot6=slot6, pbf6=pbf6:
              t.transpose(pbf6[:, slot6 * 128:(slot6 + 1) * 128], s_qpost[:, hh * 128:(hh + 1) * 128], s_ident[:]),
              reads=[b_qpost, b_consts, b_bank[6]], writes=[])
            if wname == "q":
                h = oc * 4 + hh
                dst = s_qT[:, h * 512 + mloc * 128: h * 512 + (mloc + 1) * 128]
                bdst = b_qT
            else:
                kv = hh
                dst = s_kT[:, kv * T + m * 128: kv * T + (m + 1) * 128]
                bdst = b_kT
            E("act", lambda a, dst=dst, slot6=slot6, pbf6=pbf6:
              a.activation(out=dst, in_=pbf6[:, slot6 * 128:(slot6 + 1) * 128], func=AF.Copy),
              reads=[b_bank[6]], writes=[bdst])
        P.tag = _oldtag

    def pm_bank(mloc):
        return b_bank[2 + mloc]

    def emit_attention_quarter(q):
        units = []
        for mloc in range(QM):
            for h in range(H):
                units.append((q * QM + mloc, h))
        NU = len(units)

        def scores(u, m, h):
            par = u % 2
            kh = h // 4
            nj = (m + 1) if m < 8 else 9
            kt0 = max(0, m - 8)
            base = 0 if par == 0 else 1536
            stb = banks(0, 2) if par == 0 else banks(3, 5)
            for j in range(nj):
                kt = kt0 + j
                lhsT = s_kT[:, kh * T + kt * 128: kh * T + (kt + 1) * 128]
                rhs = s_qT[:, h * 512 + (m % QM) * 128: h * 512 + (m % QM + 1) * 128]
                outp = ps[:, base + j * 128: base + (j + 1) * 128]
                E("pe", lambda t, lhsT=lhsT, rhs=rhs, outp=outp:
                  t.matmul(outp, lhsT, rhs, start=True, stop=True),
                  reads=[b_kT, b_qT] + stb, writes=[])
            return par, nj, base, stb

        def softmax_pv(u, m, h, par, nj, base, stb):
            kh = h // 4
            W_ = nj * 128
            if m >= 8:
                E("dve", lambda v, base=base:
                  v.tensor_tensor(out=ps[:, base:base + 128], in0=ps[:, base:base + 128], in1=s_mTL[:], op=ALU.add),
                  reads=[b_consts] + stb, writes=[])
            E("dve", lambda v, base=base, nj=nj:
              v.tensor_tensor(out=ps[:, base + (nj - 1) * 128: base + nj * 128],
                              in0=ps[:, base + (nj - 1) * 128: base + nj * 128], in1=s_mTR[:], op=ALU.add),
              reads=[b_consts] + stb, writes=[])
            ybank0 = 6 + (u % 2)
            yb0 = ybank0 * 512
            E("dve", lambda v, base=base, W_=W_, par=par:
              v.tensor_reduce(out=s_stat[par][:, 0:1], in_=ps[:, base:base + W_], axis=AX, op=ALU.max),
              reads=stb, writes=[b_stat[par]])
            E("pe", lambda t, par=par, yb0=yb0:
              t.matmul(ps[0:1, yb0 + 256:yb0 + 384], s_stat[par][:, 0:1], s_identf[:], start=True, stop=True),
              reads=[b_stat[par], b_consts, b_bank[ybank0]], writes=[])
            E("dve", lambda v, par=par, yb0=yb0:
              v.tensor_reduce(out=s_stat[par][0:1, 3:4], in_=ps[0:1, yb0 + 256:yb0 + 384], axis=AX, op=ALU.max),
              reads=[b_bank[ybank0]], writes=[b_stat[par]])
            E("dve", lambda v, par=par:
              v.tensor_scalar(out=s_stat[par][0:1, 4:5], in0=s_stat[par][0:1, 3:4], scalar1=-1.0, scalar2=None, op0=ALU.mult),
              reads=[b_stat[par]], writes=[b_stat[par]])
            E("pe", lambda t, par=par, yb0=yb0:
              t.matmul(ps[:, yb0 + 384:yb0 + 385], s_ones1[:], s_stat[par][0:1, 4:5], start=True, stop=True),
              reads=[b_stat[par], b_consts, b_bank[ybank0]], writes=[])
            E("dve", lambda v, par=par, yb0=yb0:
              v.tensor_copy(out=s_stat[par][:, 2:3], in_=ps[:, yb0 + 384:yb0 + 385]),
              reads=[b_bank[ybank0]], writes=[b_stat[par]])
            E("act", lambda a, base=base, W_=W_, par=par:
              a.activation(out=s_pT[par][:, 0:W_], in_=ps[:, base:base + W_], func=AF.Exp,
                           bias=s_stat[par][:, 2:3]),
              reads=[b_stat[par]] + stb, writes=[b_pT[par]])
            ybank = 6 + (u % 2)
            ybase = ybank * 512
            for j in range(nj):
                kt = max(0, m - 8) + j
                rhs = s_vaug[:, kt * 520 + kh * 130: kt * 520 + kh * 130 + 129]
                E("pe", lambda t, j=j, rhs=rhs, par=par, ybase=ybase, nj=nj, ybank=ybank:
                  t.matmul(ps[:, ybase:ybase + 129], s_pT[par][:, j * 128:(j + 1) * 128], rhs,
                           start=(j == 0), stop=(j == nj - 1)),
                  reads=[b_pT[par], b_vaug, b_bank[ybank]], writes=[])
            mpar = m % 2
            E("dve", lambda v, ybase=ybase, par=par, ybank=ybank:
              v.reciprocal(out=s_stat[par][:, 3:4], in_=ps[:, ybase + 128:ybase + 129]),
              reads=[b_bank[ybank]], writes=[b_stat[par]])
            E("dve", lambda v, ybase=ybase, par=par, h=h, mpar=mpar, ybank=ybank:
              v.tensor_scalar(out=s_ym[mpar][:, h * 128:(h + 1) * 128], in0=ps[:, ybase:ybase + 128],
                              scalar1=s_stat[par][:, 3:4], scalar2=None, op0=ALU.mult),
              reads=[b_bank[ybank], b_stat[par]], writes=[b_ym[mpar]])

        pending = None
        for u in range(NU + 1):
            if u < NU:
                m, h = units[u]
                info = scores(u, m, h)
            if pending is not None:
                softmax_pv(*pending)
            if u < NU:
                pending = (u, units[u][0], units[u][1]) + info
                if units[u][1] == 0:
                    m = units[u][0]
                    sl = m % 2
                    E("sp", lambda s, m=m, sl=sl:
                      s.dma_start(out=s_xt[sl][:], in_=d_x[m * 128:(m + 1) * 128, :]),
                      writes=[b_xe[sl]], kind="dma")
            else:
                pending = None
            if u >= 1 and units[u - 1][1] == H - 1:
                emit_attn_epilogue(units[u - 1][0], q)

    def emit_attn_epilogue(m, q):
        mloc = m % QM
        mpar = m % 2
        E("dve", lambda v, mpar=mpar, mloc=mloc:
          v.tensor_tensor(out=s_ym[mpar][:], in0=s_ym[mpar][:], in1=s_sg[:, mloc * D:(mloc + 1) * D], op=ALU.mult),
          reads=[b_ym[mpar], b_sg], writes=[b_ym[mpar]])
        for h in range(H):
            E("dve", lambda v, mpar=mpar, h=h:
              v.scalar_tensor_tensor(out=s_dump[:].bitcast(F32)[:, 0:128], in0=s_ym[mpar][:, h * 128:(h + 1) * 128],
                                     scalar=1.0, in1=s_ym[mpar][:, h * 128:(h + 1) * 128],
                                     op0=ALU.mult, op1=ALU.mult, accum_out=s_ms[:, 16 + h:17 + h]),
              reads=[b_ym[mpar]], writes=[b_dump, b_ms])
        E("dve", lambda v: v.tensor_scalar(out=s_ms[:, 16:32], in0=s_ms[:, 16:32], scalar1=1.0 / DH,
                                           scalar2=EPS_RMS, op0=ALU.mult, op1=ALU.add),
          reads=[b_ms], writes=[b_ms])
        E("act", lambda a: a.activation(out=s_ms[:, 32:48], in_=s_ms[:, 16:32], func=AF.Ln),
          reads=[b_ms], writes=[b_ms])
        E("act", lambda a: a.activation(out=s_ms[:, 32:48], in_=s_ms[:, 32:48], func=AF.Exp, scale=s_cb[:, 0:1]),
          reads=[b_ms, b_consts], writes=[b_ms])
        for h in range(H):
            E("dve", lambda v, mpar=mpar, h=h:
              v.tensor_scalar(out=s_x1st[mpar][:, h * 128:(h + 1) * 128], in0=s_ym[mpar][:, h * 128:(h + 1) * 128],
                              scalar1=s_ms[:, 32 + h:33 + h], scalar2=None, op0=ALU.mult),
              reads=[b_ym[mpar], b_ms], writes=[b_x1st[mpar]])
        E("dve", lambda v, mpar=mpar:
          v.tensor_tensor(out=s_x1st[mpar][:].rearrange("p (h d) -> p h d", h=16),
                          in0=s_x1st[mpar][:].rearrange("p (h d) -> p h d", h=16),
                          in1=s_gnw[:, :].rearrange("p (one d) -> p one d", one=1).broadcast_to([128, 16, 128]), op=ALU.mult),
          reads=[b_x1st[mpar], b_consts], writes=[b_x1st[mpar]])
        E("dve", lambda v, mpar=mpar:
          v.tensor_tensor(out=s_x1st[mpar][:], in0=s_x1st[mpar][:], in1=s_xt[mpar][:], op=ALU.add),
          reads=[b_x1st[mpar], b_xe[mpar]], writes=[b_x1st[mpar]])
        E("gp", lambda g, m=m, mpar=mpar:
          g.dma_start(out=d_x1[m * 128:(m + 1) * 128, :], in_=s_x1st[mpar][:]),
          reads=[b_x1st[mpar]], writes=[b_x1hbm], kind="dma")

    def emit_dequant_gu_tile(tl):
        E("sp", lambda s, tl=tl:
          s.dma_start(out=s_dqp[0][:, 0:256], in_=d_gup[tl * 128:(tl + 1) * 128, :]),
          writes=[b_dq], kind="dma")
        E("sp", lambda s, tl=tl:
          s.dma_start(out=s_dqs16[:, 0:16], in_=d_gus[tl * 128:(tl + 1) * 128, :]),
          writes=[b_dq], kind="dma")
        E("dve", lambda v: v.tensor_copy(out=s_dqs[:, 0:16], in_=s_dqs16[:, 0:16]),
          reads=[b_dq], writes=[b_dq])
        for half in range(2):
            emit_dequant_core(s_dqp[0][:, half * 128:(half + 1) * 128], 128, s_dqv, 1024)
            for gl in range(8):
                g = half * 8 + gl
                E("dve", lambda v, g=g, gl=gl:
                  v.tensor_scalar(out=s_dqb[:, g * 128:(g + 1) * 128], in0=s_dqv[:, gl * 128:(gl + 1) * 128],
                                  scalar1=s_dqs[:, g:g + 1], scalar2=None, op0=ALU.mult),
                  reads=[b_dq], writes=[b_dq])
        pbf7 = ps[:, 3584 + 0:4096].bitcast(BF16)
        for dc in range(16):
            slot = dc % 8
            E("pe", lambda t, dc=dc, slot=slot, pbf7=pbf7:
              t.transpose(pbf7[:, slot * 128:(slot + 1) * 128], s_dqb[:, dc * 128:(dc + 1) * 128], s_ident[:]),
              reads=[b_dq, b_consts, b_bank[7]], writes=[])
            E("act", lambda a, dc=dc, slot=slot, pbf7=pbf7:
              a.activation(out=s_stg[:, dc * 128:(dc + 1) * 128], in_=pbf7[:, slot * 128:(slot + 1) * 128], func=AF.Copy),
              reads=[b_bank[7]], writes=[b_stg])
        E("gp", lambda g, tl=tl:
          g.dma_start(out=d_agin_gu[tl, :, :], in_=s_stg[:]),
          reads=[b_stg], writes=[b_agin_gu], kind="dma")

    def emit_dequant_core(src_i32, ncols, dstf32, nd):
        t1, m3, s8 = s_dqt[0], s_dqt[1], s_dqt[2]
        mf, uu = s_dqf[0], s_dqf[1]
        for j in range(8):
            if j == 0:
                E("dve", lambda v, src_i32=src_i32, ncols=ncols:
                  v.tensor_copy(out=t1[:, 0:ncols], in_=src_i32[:, 0:ncols]),
                  reads=[b_dq], writes=[b_dq])
            else:
                E("dve", lambda v, src_i32=src_i32, ncols=ncols, j=j:
                  v.tensor_single_scalar(out=t1[:, 0:ncols], in_=src_i32[:, 0:ncols], scalar=4 * j,
                                         op=ALU.logical_shift_right),
                  reads=[b_dq], writes=[b_dq])
            E("dve", lambda v, ncols=ncols: v.tensor_single_scalar(out=m3[:, 0:ncols], in_=t1[:, 0:ncols], scalar=7, op=ALU.bitwise_and),
              reads=[b_dq], writes=[b_dq])
            E("dve", lambda v, ncols=ncols: v.tensor_single_scalar(out=s8[:, 0:ncols], in_=t1[:, 0:ncols], scalar=8, op=ALU.bitwise_and),
              reads=[b_dq], writes=[b_dq])
            E("dve", lambda v, ncols=ncols: v.tensor_copy(out=mf[:, 0:ncols], in_=m3[:, 0:ncols]),
              reads=[b_dq], writes=[b_dq])
            E("dve", lambda v, ncols=ncols: v.tensor_single_scalar(out=m3[:, 0:ncols], in_=m3[:, 0:ncols], scalar=22, op=ALU.logical_shift_left),
              reads=[b_dq], writes=[b_dq])
            E("dve", lambda v, ncols=ncols: v.tensor_single_scalar(out=m3[:, 0:ncols], in_=m3[:, 0:ncols], scalar=0x3F000000, op=ALU.add),
              reads=[b_dq], writes=[b_dq])
            E("dve", lambda v, ncols=ncols: v.tensor_scalar(out=uu[:, 0:ncols], in0=mf[:, 0:ncols], scalar1=-0.25, scalar2=0.5,
                                                            op0=ALU.mult, op1=ALU.add),
              reads=[b_dq], writes=[b_dq])
            E("dve", lambda v, ncols=ncols: v.tensor_single_scalar(out=uu[:, 0:ncols], in_=uu[:, 0:ncols], scalar=0.0, op=ALU.max),
              reads=[b_dq], writes=[b_dq])
            E("dve", lambda v, ncols=ncols: v.tensor_tensor(out=mf[:, 0:ncols], in0=m3[:, 0:ncols].bitcast(F32), in1=uu[:, 0:ncols], op=ALU.subtract),
              reads=[b_dq], writes=[b_dq])
            E("dve", lambda v, ncols=ncols: v.tensor_single_scalar(out=s8[:, 0:ncols], in_=s8[:, 0:ncols], scalar=28, op=ALU.logical_shift_left),
              reads=[b_dq], writes=[b_dq])
            E("dve", lambda v, ncols=ncols, j=j, dstf32=dstf32, nd=nd:
              v.tensor_tensor(out=dstf32[:, 0:nd].rearrange("p (c e) -> p c e", e=8)[:, :, j:j + 1].bitcast(I32),
                              in0=mf[:, 0:ncols].bitcast(I32).rearrange("p (c one) -> p c one", one=1),
                              in1=s8[:, 0:ncols].rearrange("p (c one) -> p c one", one=1), op=ALU.bitwise_or),
              reads=[b_dq], writes=[b_dq])

    def emit_dequant_dn():
        for ot in range(2):
            E("sp", lambda s, ot=ot:
              s.dma_start(out=s_dqp[0][:, 0:704], in_=d_dnp[ot * 128:(ot + 1) * 128, :]),
              writes=[b_dq], kind="dma")
            E("sp", lambda s, ot=ot:
              s.dma_start(out=s_dqs16[:, 0:NIT], in_=d_dns[ot * 128:(ot + 1) * 128, :]),
              writes=[b_dq], kind="dma")
            E("dve", lambda v: v.tensor_copy(out=s_dqs[:, 0:NIT], in_=s_dqs16[:, 0:NIT]),
              reads=[b_dq], writes=[b_dq])
            for half in range(4):
                sub = s_dqp[0][:, half * 176:(half + 1) * 176]
                emit_dequant_core(sub, 176, s_dqv, 1408)
                for gl in range(11):
                    g = half * 11 + gl
                    E("dve", lambda v, gl=gl, g=g:
                      v.tensor_scalar(out=s_dqb[:, gl * 128:(gl + 1) * 128], in0=s_dqv[:, gl * 128:(gl + 1) * 128],
                                      scalar1=s_dqs[:, g:g + 1], scalar2=None, op0=ALU.mult),
                      reads=[b_dq], writes=[b_dq])
                pbf7 = ps[:, 3584:4096].bitcast(BF16)
                for gl in range(11):
                    slot = gl % 8
                    E("pe", lambda t, gl=gl, slot=slot, pbf7=pbf7:
                      t.transpose(pbf7[:, slot * 128:(slot + 1) * 128], s_dqb[:, gl * 128:(gl + 1) * 128], s_ident[:]),
                      reads=[b_dq, b_consts, b_bank[7]], writes=[])
                    E("act", lambda a, gl=gl, slot=slot, pbf7=pbf7:
                      a.activation(out=s_stgd[:, gl * 128:(gl + 1) * 128],
                                   in_=pbf7[:, slot * 128:(slot + 1) * 128], func=AF.Copy),
                      reads=[b_bank[7]], writes=[b_stgd])
                it0 = half * 11
                E("gp", lambda g, ot=ot, it0=it0:
                  g.dma_start(out=d_agin_dn[ot, :, it0:it0 + 11, :],
                              in_=s_stgd[:].rearrange("p (it ds) -> p it ds", it=11)),
                  reads=[b_stgd], writes=[b_agin_dn], kind="dma")

    P.tag = "p1"
    emit_ln_quarter(0, d_x, None, EPS_LN)
    emit_proj_quarter(0)
    if debug:
        E("gp", lambda g: g.dma_start(out=d_dbg_act[0, :, 0:8192], in_=s_lnT[:]), reads=[b_lnT], writes=[], kind="dma")
        E("gp", lambda g: g.dma_start(out=d_dbg_act[1, :, 0:8192], in_=s_qT[:]), reads=[b_qT], writes=[], kind="dma")
        E("gp", lambda g: g.dma_start(out=d_dbg_act[2, :, 0:8192], in_=s_kT[:]), reads=[b_kT], writes=[], kind="dma")
        E("gp", lambda g: g.dma_start(out=d_dbg_act[3, :, 0:8320], in_=s_vaug[:]), reads=[b_vaug], writes=[], kind="dma")
        E("gp", lambda g: g.dma_start(out=d_dbg_act[4, :, 0:8192], in_=s_sg[:]), reads=[b_sg], writes=[], kind="dma")
    P.tag = "p2"
    emit_attention_quarter(0)
    P.tag = "dq"
    for tl in range(0, 4):
        emit_dequant_gu_tile(tl)
    P.tag = "p1"
    emit_ln_quarter(1, d_x, None, EPS_LN)
    emit_proj_quarter(1)
    P.tag = "p2"
    emit_attention_quarter(1)
    P.tag = "dq"
    for tl in range(4, 8):
        emit_dequant_gu_tile(tl)
    P.tag = "p1"
    emit_ln_quarter(2, d_x, None, EPS_LN)
    emit_proj_quarter(2)
    P.tag = "p2"
    emit_attention_quarter(2)
    P.tag = "dq"
    for tl in range(8, 11):
        emit_dequant_gu_tile(tl)
    emit_dequant_dn()
    E("gp", lambda g: g.collective_compute(
        "AllGather", ALU.bypass, replica_groups=[list(range(NCORE))],
        ins=[d_agin_gu.opt()], outs=[d_agout_gu.opt()]),
      reads=[b_agin_gu], writes=[b_agout], kind="coll")
    E("gp", lambda g: g.collective_compute(
        "AllGather", ALU.bypass, replica_groups=[list(range(NCORE))],
        ins=[d_agin_dn.opt()], outs=[d_agout_dn.opt()]),
      reads=[b_agin_dn], writes=[b_agout], kind="coll")
    P.tag = "p1"
    emit_ln_quarter(3, d_x, None, EPS_LN)
    emit_proj_quarter(3)
    if debug:
        E("gp", lambda g: g.dma_start(out=d_dbg_act[5, :, 0:8192], in_=s_lnT[:]), reads=[b_lnT], writes=[], kind="dma")
        E("gp", lambda g: g.dma_start(out=d_dbg_act[6, :, 0:8192], in_=s_qT[:]), reads=[b_qT], writes=[], kind="dma")
        E("gp", lambda g: g.dma_start(out=d_dbg_act[7, :, 0:8192], in_=s_kT[:]), reads=[b_kT], writes=[], kind="dma")
        E("gp", lambda g: g.dma_start(out=d_dbg_act[8, :, 0:8320], in_=s_vaug[:]), reads=[b_vaug], writes=[], kind="dma")
        E("gp", lambda g: g.dma_start(out=d_dbg_act[9, :, 0:8192], in_=s_sg[:]), reads=[b_sg], writes=[], kind="dma")
    P.tag = "p2"
    emit_attention_quarter(3)
    P.tag = "other"

    if debug:
        E("gp", lambda g: g.dma_start(out=d_dbg_gu, in_=d_agin_gu), reads=[b_agin_gu], writes=[], kind="dma")
        E("gp", lambda g: g.dma_start(out=d_dbg_dn, in_=d_agin_dn), reads=[b_agin_dn], writes=[], kind="dma")
        E("gp", lambda g: g.dma_start(out=d_dbg_wbf, in_=d_wbf), reads=[b_wbf["q"], b_wbf["k"], b_wbf["v"], b_wbf["g"]], writes=[], kind="dma")
        E("gp", lambda g: g.dma_start(out=d_dbg_x1, in_=d_x1), reads=[b_x1hbm], writes=[], kind="dma")

    P.in_ffn = True
    P.barrier()
    es.close()
    es2 = ExitStack()

    def sb2(name, shape, dt):
        return es2.enter_context(nc.sbuf_tensor(name, shape, dt))

    f_ln2T = sb2("ln2T", [128, 16 * FB], BF16)
    f_hT = sb2("hT", [128, NIT * FB], BF16)
    f_wgu = [sb2(f"wgu{i}", [128, 2048], BF16) for i in range(4)]
    f_stmp = sb2("stmp", [128, FB], F32)
    f_wdp = [sb2(f"wdp{i}", [128, NIT * 256], BF16) for i in range(2)]
    f_y = sb2("fy", [128, D], F32)
    f_x1 = [sb2(f"fx1_{i}", [128, D], F32) for i in range(2)]
    f_xt = [sb2(f"fxt{i}", [128, D], F32) for i in range(2)]
    f_dump = sb2("fdump", [128, D], BF16)
    f_lnst = sb2("flnst", [128, 16], F32)
    f_cb = sb2("fcb", [128, 4], F32)
    f_ident = sb2("fident", [128, 128], BF16)
    f_stat = sb2("fstat", [128, 8], F32)

    b_ln2T = Buf("ln2T")
    b_hT = Buf("hT")
    b_fwgu = [Buf(f"fwgu{i}") for i in range(4)]
    b_fstmp = Buf("fstmp")
    b_fwdp = [Buf(f"fwdp{i}") for i in range(2)]
    b_fy = Buf("fy")
    b_fx1 = [Buf(f"fx1_{i}") for i in range(2)]
    b_fxt = [Buf(f"fxt{i}") for i in range(2)]
    b_fdump = Buf("fdump")
    b_flnst = Buf("flnst")
    b_fconsts = Buf("fconsts")
    b_fstat = Buf("fstat")

    E("dve", lambda v: v.memset(f_cb[:, 0:1], -0.5), writes=[b_fconsts])
    E("gp", lambda g: g.dma_start(out=f_ident[:], in_=d_ident), writes=[b_fconsts], kind="dma")

    def emit_ln2_block(blk):
        for mloc in range(QM):
            m = blk * QM + mloc
            sl = mloc % 2
            E("sp", lambda s, m=m, sl=sl:
              s.dma_start(out=f_xt[sl][:], in_=d_x1[m * 128:(m + 1) * 128, :]),
              reads=[b_x1hbm], writes=[b_fxt[sl]], kind="dma")
            E("dve", lambda v, sl=sl, mloc=mloc:
              v.tensor_reduce(out=f_lnst[:, mloc:mloc + 1], in_=f_xt[sl][:], axis=AX, op=ALU.add),
              reads=[b_fxt[sl]], writes=[b_flnst])
            E("act", lambda a, sl=sl, mloc=mloc:
              a.activation(out=f_dump[:], in_=f_xt[sl][:], func=AF.Square,
                           accum_out=f_lnst[:, 4 + mloc:5 + mloc]),
              reads=[b_fxt[sl]], writes=[b_fdump, b_flnst])
            E("dve", lambda v, mloc=mloc:
              v.tensor_scalar(out=f_lnst[:, 8 + mloc:9 + mloc], in0=f_lnst[:, mloc:mloc + 1],
                              scalar1=1.0 / D, scalar2=None, op0=ALU.mult),
              reads=[b_flnst], writes=[b_flnst])
            E("dve", lambda v, mloc=mloc:
              v.scalar_tensor_tensor(out=f_lnst[:, 12 + mloc:13 + mloc], in0=f_lnst[:, 8 + mloc:9 + mloc],
                                     scalar=-1.0, in1=f_lnst[:, 8 + mloc:9 + mloc], op0=ALU.mult, op1=ALU.mult),
              reads=[b_flnst], writes=[b_flnst])
            E("dve", lambda v, mloc=mloc:
              v.scalar_tensor_tensor(out=f_lnst[:, 12 + mloc:13 + mloc], in0=f_lnst[:, 4 + mloc:5 + mloc],
                                     scalar=1.0 / D, in1=f_lnst[:, 12 + mloc:13 + mloc], op0=ALU.mult, op1=ALU.add),
              reads=[b_flnst], writes=[b_flnst])
            E("dve", lambda v, mloc=mloc:
              v.tensor_scalar(out=f_lnst[:, 12 + mloc:13 + mloc], in0=f_lnst[:, 12 + mloc:13 + mloc],
                              scalar1=1.0, scalar2=EPS_LN, op0=ALU.mult, op1=ALU.add),
              reads=[b_flnst], writes=[b_flnst])
            E("act", lambda a, mloc=mloc: a.activation(out=f_lnst[:, 12 + mloc:13 + mloc], in_=f_lnst[:, 12 + mloc:13 + mloc], func=AF.Ln),
              reads=[b_flnst], writes=[b_flnst])
            E("act", lambda a, mloc=mloc: a.activation(out=f_lnst[:, 12 + mloc:13 + mloc], in_=f_lnst[:, 12 + mloc:13 + mloc], func=AF.Exp, scale=f_cb[:, 0:1]),
              reads=[b_flnst, b_fconsts], writes=[b_flnst])
            E("dve", lambda v, mloc=mloc:
              v.scalar_tensor_tensor(out=f_lnst[:, 8 + mloc:9 + mloc], in0=f_lnst[:, 8 + mloc:9 + mloc],
                                     scalar=-1.0, in1=f_lnst[:, 12 + mloc:13 + mloc], op0=ALU.mult, op1=ALU.mult),
              reads=[b_flnst], writes=[b_flnst])

            E("act", lambda a, sl=sl, mloc=mloc:
              a.activation(out=f_dump[:], in_=f_xt[sl][:], func=AF.Identity,
                           bias=f_lnst[:, 8 + mloc:9 + mloc], scale=f_lnst[:, 12 + mloc:13 + mloc]),
              reads=[b_fxt[sl], b_flnst], writes=[b_fdump])
            pbf = ps[:, 0:1024].bitcast(BF16)
            for dc in range(16):
                E("pe", lambda t, dc=dc, pbf=pbf:
                  t.transpose(pbf[:, dc * 128:(dc + 1) * 128], f_dump[:, dc * 128:(dc + 1) * 128], f_ident[:]),
                  reads=[b_fdump, b_fconsts] + banks(0, 1), writes=[])
            dst = f_ln2T[:, :].rearrange("p (dc t) -> p dc t", dc=16)[:, :, mloc * 128:(mloc + 1) * 128]
            E("act", lambda a, dst=dst, pbf=pbf:
              a.activation(out=dst, in_=pbf.rearrange("p (dc t) -> p dc t", dc=16), func=AF.Copy),
              reads=banks(0, 1), writes=[b_ln2T])

    def emit_gup_block(blk):
        for Tt in range(NIT):
            sg_ = Tt % 2
            su_ = 2 + Tt % 2
            Tu = Tt + NIT
            E("sp", lambda s, Tt=Tt, sg_=sg_:
              s.dma_start(out=f_wgu[sg_][:], in_=d_agout_gu[Tt, :, :]),
              reads=[b_agout], writes=[b_fwgu[sg_]], kind="dma")
            E("sp", lambda s, Tu=Tu, su_=su_:
              s.dma_start(out=f_wgu[su_][:], in_=d_agout_gu[Tu, :, :]),
              reads=[b_agout], writes=[b_fwgu[su_]], kind="dma")
            gb = 2 + (Tt % 2) * 2
            ub = gb + 1
            pg = ps[:, gb * 512:(gb + 1) * 512]
            pu = ps[:, ub * 512:(ub + 1) * 512]
            for dc in range(16):
                E("pe", lambda t, dc=dc, sg_=sg_, pg=pg:
                  t.matmul(pg, f_wgu[sg_][:, dc * 128:(dc + 1) * 128], f_ln2T[:, dc * FB:(dc + 1) * FB],
                           start=(dc == 0), stop=(dc == 15)),
                  reads=[b_fwgu[sg_], b_ln2T, b_bank[gb]], writes=[])
            for dc in range(16):
                E("pe", lambda t, dc=dc, su_=su_, pu=pu:
                  t.matmul(pu, f_wgu[su_][:, dc * 128:(dc + 1) * 128], f_ln2T[:, dc * FB:(dc + 1) * FB],
                           start=(dc == 0), stop=(dc == 15)),
                  reads=[b_fwgu[su_], b_ln2T, b_bank[ub]], writes=[])
            E("act", lambda a, pg=pg: a.activation(out=f_stmp[:], in_=pg, func=AF.Silu),
              reads=[b_bank[gb]], writes=[b_fstmp])
            E("dve", lambda v, pu=pu, Tt=Tt:
              v.tensor_tensor(out=f_hT[:, Tt * FB:(Tt + 1) * FB], in0=f_stmp[:], in1=pu, op=ALU.mult),
              reads=[b_fstmp, b_bank[ub]], writes=[b_hT])

    def emit_ffn_block(blk):
        emit_ln2_block(blk)
        emit_gup_block(blk)
        for dcq in range(8):
            wsl = dcq % 2
            for ot in range(2):
                dst = f_wdp[wsl][:, :].rearrange("p (it d) -> p it d", it=NIT)[:, :, ot * 128:(ot + 1) * 128]
                E("sp", lambda s, dcq=dcq, ot=ot, dst=dst:
                  s.dma_start(out=dst, in_=d_agout_dn[dcq, ot, :, :, :]),
                  reads=[b_agout], writes=[b_fwdp[wsl]], kind="dma")
            for tt in range(4):
                yb = 2 + tt
                py = ps[:, yb * 512: yb * 512 + 256]
                for it in range(NIT):
                    E("pe", lambda t, it=it, tt=tt, wsl=wsl, py=py:
                      t.matmul(py, f_hT[:, it * FB + tt * 128: it * FB + (tt + 1) * 128],
                               f_wdp[wsl][:, it * 256:(it + 1) * 256],
                               start=(it == 0), stop=(it == NIT - 1)),
                      reads=[b_hT, b_fwdp[wsl], b_bank[yb]], writes=[])
            for tt in range(4):
                yb = 2 + tt
                py = ps[:, yb * 512: yb * 512 + 256]
                tpar = tt % 2
                E("dve", lambda v, py=py, dcq=dcq, tt=tt, tpar=tpar:
                  v.tensor_copy(out=(f_x1[tpar] if tt < 2 else f_xt[tpar])[:, dcq * 256:(dcq + 1) * 256], in_=py),
                  reads=[b_bank[yb]], writes=[b_fx1[tpar] if tt < 2 else b_fxt[tpar]])
        for tt in range(4):
            m = blk * 4 + tt
            tpar = tt % 2
            ybuf = f_x1[tpar] if tt < 2 else f_xt[tpar]
            bybuf = b_fx1[tpar] if tt < 2 else b_fxt[tpar]
            E("dve", lambda v, ybuf=ybuf, bybuf=bybuf, tt=tt:
              v.scalar_tensor_tensor(out=f_dump[:, 0:D].bitcast(F32)[:, 0:1024], in0=ybuf[:, 0:1024], scalar=1.0,
                                     in1=ybuf[:, 0:1024], op0=ALU.mult, op1=ALU.mult,
                                     accum_out=f_stat[:, 0:1]),
              reads=[bybuf], writes=[b_fdump, b_fstat])
            E("dve", lambda v, ybuf=ybuf, bybuf=bybuf, tt=tt:
              v.scalar_tensor_tensor(out=f_dump[:, 0:D].bitcast(F32)[:, 0:1024], in0=ybuf[:, 1024:2048], scalar=1.0,
                                     in1=ybuf[:, 1024:2048], op0=ALU.mult, op1=ALU.mult,
                                     accum_out=f_stat[:, 1:2]),
              reads=[bybuf], writes=[b_fdump, b_fstat])
            E("dve", lambda v:
              v.tensor_tensor(out=f_stat[:, 2:3], in0=f_stat[:, 0:1], in1=f_stat[:, 1:2], op=ALU.add),
              reads=[b_fstat], writes=[b_fstat])
            E("dve", lambda v:
              v.tensor_scalar(out=f_stat[:, 2:3], in0=f_stat[:, 2:3], scalar1=1.0 / D, scalar2=EPS_RMS,
                              op0=ALU.mult, op1=ALU.add),
              reads=[b_fstat], writes=[b_fstat])
            E("act", lambda a: a.activation(out=f_stat[:, 3:4], in_=f_stat[:, 2:3], func=AF.Ln),
              reads=[b_fstat], writes=[b_fstat])
            E("act", lambda a: a.activation(out=f_stat[:, 3:4], in_=f_stat[:, 3:4], func=AF.Exp, scale=f_cb[:, 0:1]),
              reads=[b_fstat, b_fconsts], writes=[b_fstat])
            E("sp", lambda s, m=m:
              s.dma_start(out=f_y[:], in_=d_x1[m * 128:(m + 1) * 128, :]),
              reads=[b_x1hbm], writes=[b_fy], kind="dma")
            E("dve", lambda v, ybuf=ybuf, bybuf=bybuf:
              v.tensor_scalar(out=ybuf[:], in0=ybuf[:], scalar1=f_stat[:, 3:4], scalar2=None, op0=ALU.mult),
              reads=[bybuf, b_fstat], writes=[bybuf])
            E("dve", lambda v, ybuf=ybuf, bybuf=bybuf:
              v.tensor_tensor(out=ybuf[:], in0=ybuf[:], in1=f_y[:], op=ALU.add),
              reads=[bybuf, b_fy], writes=[bybuf])
            E("gp", lambda g, m=m, ybuf=ybuf, bybuf=bybuf:
              g.dma_start(out=d_out[m * 128:(m + 1) * 128, :], in_=ybuf[:]),
              reads=[bybuf], writes=[b_outhbm], kind="dma")

    for blk in range(NFB):
        emit_ffn_block(blk)

    sems = {}
    dma_sems = {}
    sem_stack = ExitStack()
    for e in Prog.ENGS:
        sems[e] = sem_stack.enter_context(nc.semaphore(f"sem_{e}"))
    for e in Prog.ENGS:
        dma_sems[e] = sem_stack.enter_context(nc.semaphore(f"dsem_{e}"))
    cc_sem = sem_stack.enter_context(nc.semaphore("ccsem"))

    with sem_stack, es2, nc.Block() as block:
        run_stream = P.resolve_and_replay(nc, None, sems, dma_sems, cc_sem)

        @block.tensor
        def _(tensor):
            run_stream("pe", tensor)

        @block.vector
        def _(vector):
            run_stream("dve", vector)

        @block.scalar
        def _(scalar):
            run_stream("act", scalar)

        @block.sync
        def _(sync):
            run_stream("sp", sync)
            sync.wait_ge(dma_sems["gp"], 16 * P.total_dma["gp"])
            sync.wait_ge(dma_sems["sp"], 16 * P.total_dma["sp"])

        @block.gpsimd
        def _(gpsimd):
            run_stream("gp", gpsimd)

    return nc


_NC_CACHE = None


def _host_constants():
    inv_freq = 1.0 / (10000.0 ** (np.arange(0, DH, 2, dtype=np.float64) / DH))
    t = np.arange(T, dtype=np.float64)
    fr = np.outer(t, inv_freq)
    cos = np.cos(fr).astype(np.float32)
    sin = np.sin(fr).astype(np.float32)
    cosb = np.zeros((128, NM * 64), np.float32)
    sinb = np.zeros((128, NM * 64), np.float32)
    for m in range(NM):
        cosb[:, m * 64:(m + 1) * 64] = cos[m * 128:(m + 1) * 128]
        sinb[:, m * 64:(m + 1) * 64] = sin[m * 128:(m + 1) * 128]
    logp = np.zeros((128, NM), np.float32)
    for m in range(NM):
        logp[:, m] = np.log(np.arange(m * 128 + 1, m * 128 + 129, dtype=np.float64)).astype(np.float32)
    ident = np.eye(128, dtype=np.float32).astype(ml_dtypes.bfloat16)
    identf = np.eye(128, dtype=np.float32)
    onesr = np.ones((1, 128), dtype=np.float32)
    p = np.arange(128)[:, None]
    qq = np.arange(128)[None, :]
    mTL = np.where(p >= qq, 0.0, NEG).astype(np.float32)
    mTR = np.where(p <= qq, 0.0, NEG).astype(np.float32)
    return cosb, sinb, logp, ident, identf, onesr, mTL, mTR


def kernel(*, _debug_hook=None, **_kw):
    return _kernel_impl(**_kw)


def _kernel_impl(L0_attn_hidden_states, L0_attn_Wq, L0_attn_Wk, L0_attn_Wv, L0_attn_Wg,
           L0_attn_softmax_scaler, L0_attn_gn_weight,
           L0_ffn_gate_up_packed, L0_ffn_gate_up_scales, L0_ffn_down_packed, L0_ffn_down_scales):
    import os
    global _NC_CACHE
    if _NC_CACHE is None:
        _NC_CACHE = build_nc(debug=bool(os.environ.get("BASSDBG")))
    nc = _NC_CACHE

    def _np(a, dt=None):
        a = np.asarray(a)
        if dt is not None and a.dtype != dt:
            a = a.astype(dt)
        return a

    x = _np(L0_attn_hidden_states, np.float32)
    wq = np.ascontiguousarray(_np(L0_attn_Wq, np.float32))
    wk = np.ascontiguousarray(_np(L0_attn_Wk, np.float32))
    wv = np.ascontiguousarray(_np(L0_attn_Wv, np.float32))
    wg = np.ascontiguousarray(_np(L0_attn_Wg, np.float32))
    scaler = _np(L0_attn_softmax_scaler, np.float32)
    gnw = _np(L0_attn_gn_weight, np.float32)
    gup = _np(L0_ffn_gate_up_packed, np.int32)
    gus = _np(L0_ffn_gate_up_scales)
    dnp = _np(L0_ffn_down_packed, np.int32)
    dns = _np(L0_ffn_down_scales)

    gus_t = np.ascontiguousarray(gus.T)
    dns_t = np.ascontiguousarray(dns.T)
    scaler_b = np.ascontiguousarray(np.broadcast_to(scaler[None, :], (128, H)))
    gnw_b = np.ascontiguousarray(np.broadcast_to(gnw[None, :], (128, 128)))
    cosb, sinb, logp, ident, identf, onesr, mTL, mTR = _host_constants()

    in_maps = []
    for c in range(NCORE):
        in_maps.append({
            "x": np.ascontiguousarray(x[c]),
            "wq": wq, "wk": wk, "wv": wv, "wg": wg,
            "gup_packed_sl": np.ascontiguousarray(gup[c * 1408:(c + 1) * 1408]),
            "gu_scales_sl": np.ascontiguousarray(gus_t[c * 1408:(c + 1) * 1408]),
            "dn_packed_sl": np.ascontiguousarray(dnp[c * DN_ROWS:(c + 1) * DN_ROWS]),
            "dn_scales_sl": np.ascontiguousarray(dns_t[c * DN_ROWS:(c + 1) * DN_ROWS]),
            "scaler_b": scaler_b, "gnw_b": gnw_b,
            "cosb": cosb, "sinb": sinb, "logpos": logp,
            "ident": ident, "identf": identf, "onesr": onesr, "mTL": mTL, "mTR": mTR,
        })
    res = run_bass_kernel_spmd(nc, in_maps, core_ids=list(range(NCORE)))
    out = np.stack([np.asarray(r["out"], dtype=np.float32) for r in res.results], axis=0)
    import os as _os
    if _os.environ.get("BASSDBG"):
        kernel._dbg = res.results
    return out


# version 07
# speedup vs baseline: 1.1315x; 1.1315x over previous
import math
from contextlib import ExitStack

import numpy as np
import ml_dtypes

import concourse.bass as bass
import concourse.mybir as mybir
from concourse.bass_utils import run_bass_kernel_spmd
from concourse.bass_isa import ReduceOp
from concourse.library_config import all_libraries

F32 = mybir.dt.float32
BF16 = mybir.dt.bfloat16
F16 = mybir.dt.float16
I32 = mybir.dt.int32
AF = mybir.ActivationFunctionType
ALU = mybir.AluOpType
AX = mybir.AxisListType.X
LIB_ATTN = [l for l in all_libraries if l.name == "attn"][0]

T = 2048
D = 2048
H = 16
KVH = 4
DH = 128
WINDOW = 1024
NM = 16
QM = 4
NQ = 4
IO = 11264
I5 = 5632
NIT = 44
GUT = 88
GUT_SL = 11
DN_ROWS = 256
NCORE = 8
FB = 512
NFB = T // FB
EPS_LN = 1e-5
EPS_RMS = 1e-6
NEG = -1e30


class Buf:
    __slots__ = ("name", "excl", "w", "readers")

    def __init__(self, name, excl=False):
        self.name = name
        self.excl = excl
        self.w = None
        self.readers = []


class Prog:
    ENGS = ("pe", "dve", "act", "sp", "gp")

    def __init__(self):
        self.ops = []
        self.barrier_deps = []

    SEQ = False
    last_chain = None
    ALLDRAIN = False
    SEQTAG = ""
    tag = "other"
    SEQA = False
    SEQF = False
    in_ffn = False
    PEDRAIN = False

    def emit(self, eng, fn, reads=(), writes=(), kind="c"):
        i = len(self.ops)
        deps = set(self.barrier_deps)
        if i > 0 and (self.SEQ or (self.SEQA and not self.in_ffn) or (self.SEQF and self.in_ffn)):
            deps.add(i - 1)
        elif (self.SEQTAG and self.tag in self.SEQTAG.split(",") and kind != "dma"):
            if self.last_chain is not None:
                deps.add(self.last_chain)
            self.last_chain = i
        for b in reads:
            if b.excl:
                if b.w is not None:
                    deps.add(b.w)
                deps.update(b.readers)
                b.w = i
                b.readers = []
            else:
                if b.w is not None:
                    deps.add(b.w)
                b.readers.append(i)
        for b in writes:
            if b.w is not None:
                deps.add(b.w)
            deps.update(b.readers)
            b.w = i
            b.readers = []
        deps.discard(i)
        self.ops.append(dict(eng=eng, fn=fn, deps=deps, kind=kind, inc=False))
        return i

    def barrier(self):
        last = {}
        for i, op in enumerate(self.ops):
            last[op["eng"]] = i
        self.barrier_deps = list(last.values())

    def resolve_and_replay(self, nc, engines, sems, dma_sems, cc_sem):
        ops = self.ops
        for i, op in enumerate(ops):
            for d in op["deps"]:
                p = ops[d]
                if p["kind"] == "dma" or p["kind"] == "coll":
                    pass
                elif p["eng"] == op["eng"]:
                    pass
                else:
                    p["inc"] = True
        cnt = {e: 0 for e in self.ENGS}
        dma_cnt = {e: 0 for e in self.ENGS}
        cc_cnt = 0
        for op in ops:
            if op["kind"] == "dma":
                dma_cnt[op["eng"]] += 1
                op["done"] = ("dma", op["eng"], 16 * dma_cnt[op["eng"]])
            elif op["kind"] == "coll":
                cc_cnt += 1
                op["done"] = ("cc", None, cc_cnt)
            else:
                if op["inc"]:
                    cnt[op["eng"]] += 2 if (self.ALLDRAIN or (op["eng"] == "pe" and self.PEDRAIN)) else 1
                op["done"] = ("c", op["eng"], cnt[op["eng"]])
        self.total_dma = dict(dma_cnt)
        self.total_cc = cc_cnt
        streams = {e: [] for e in self.ENGS}
        for i, op in enumerate(ops):
            streams[op["eng"]].append(i)

        def run_stream(eng_name, eng):
            watermark = {}
            need_drain_upto = -1
            last_drain = -1
            my_pos = {}
            for pos, i in enumerate(streams[eng_name]):
                my_pos[i] = pos
            for i in streams[eng_name]:
                op = ops[i]
                for d in sorted(op["deps"]):
                    p = ops[d]
                    if p["kind"] == "dma":
                        key = ("dma", p["eng"])
                        val = p["done"][2]
                        if p["eng"] == eng_name:
                            pass
                        if watermark.get(key, 0) < val:
                            eng.wait_ge(dma_sems[p["eng"]], val)
                            watermark[key] = val
                    elif p["kind"] == "coll":
                        key = ("cc",)
                        val = p["done"][2]
                        if watermark.get(key, 0) < val:
                            eng.wait_ge(cc_sem, val)
                            watermark[key] = val
                    elif p["eng"] == eng_name:
                        if my_pos[d] > last_drain:
                            need_drain_upto = max(need_drain_upto, my_pos[d])
                    else:
                        key = ("c", p["eng"])
                        val = p["done"][2]
                        if val > 0 and watermark.get(key, 0) < val:
                            eng.wait_ge(sems[p["eng"]], val)
                            watermark[key] = val
                if need_drain_upto >= 0 and eng_name != "pe":
                    eng.drain()
                    last_drain = my_pos[i] - 1
                    need_drain_upto = -1
                if op["kind"] == "dma":
                    key = ("dma", eng_name)
                    prev = op["done"][2] - 16
                    if prev > 0 and watermark.get(key, 0) < prev:
                        eng.wait_ge(dma_sems[eng_name], prev)
                        watermark[key] = prev
                inst = op["fn"](eng)
                if op["kind"] == "dma":
                    inst.then_inc(dma_sems[eng_name], 16)
                elif op["kind"] == "coll":
                    inst.then_inc(cc_sem, 1)
                elif op["inc"]:
                    inst.then_inc(sems[eng_name], 1)
                    if self.ALLDRAIN or (eng_name == "pe" and self.PEDRAIN):
                        eng.drain().then_inc(sems[eng_name], 1)
            return None

        return run_stream


def ceil_div(a, b):
    return (a + b - 1) // b


def build_nc(debug=False):
    nc = bass.Bass("TRN2", target_bir_lowering=False)

    d_x = nc.dram_tensor("x", [T, D], F32, kind="ExternalInput").ap()
    d_wq = nc.dram_tensor("wq", [D, D], F32, kind="ExternalInput").ap()
    d_wk = nc.dram_tensor("wk", [D, 512], F32, kind="ExternalInput").ap()
    d_wv = nc.dram_tensor("wv", [D, 512], F32, kind="ExternalInput").ap()
    d_wg = nc.dram_tensor("wg", [D, D], F32, kind="ExternalInput").ap()
    d_gup = nc.dram_tensor("gup_packed_sl", [1408, 256], I32, kind="ExternalInput").ap()
    d_gus = nc.dram_tensor("gu_scales_sl", [1408, 16], F16, kind="ExternalInput").ap()
    d_dnp = nc.dram_tensor("dn_packed_sl", [DN_ROWS, 704], I32, kind="ExternalInput").ap()
    d_dns = nc.dram_tensor("dn_scales_sl", [DN_ROWS, NIT], F16, kind="ExternalInput").ap()
    d_scaler = nc.dram_tensor("scaler_b", [128, H], F32, kind="ExternalInput").ap()
    d_gnw = nc.dram_tensor("gnw_b", [128, 128], F32, kind="ExternalInput").ap()
    d_cos = nc.dram_tensor("cosb", [128, NM * 64], F32, kind="ExternalInput").ap()
    d_sin = nc.dram_tensor("sinb", [128, NM * 64], F32, kind="ExternalInput").ap()
    d_logp = nc.dram_tensor("logpos", [128, NM], F32, kind="ExternalInput").ap()
    d_ident = nc.dram_tensor("ident", [128, 128], BF16, kind="ExternalInput").ap()
    d_identf = nc.dram_tensor("identf", [128, 128], F32, kind="ExternalInput").ap()
    d_onesr = nc.dram_tensor("onesr", [1, 128], F32, kind="ExternalInput").ap()
    d_mTL = nc.dram_tensor("mTL", [128, 128], F32, kind="ExternalInput").ap()
    d_mTR = nc.dram_tensor("mTR", [128, 128], F32, kind="ExternalInput").ap()
    d_out = nc.dram_tensor("out", [T, D], F32, kind="ExternalOutput").ap()
    d_wbf = nc.dram_tensor("wbf", [D, 5120], BF16).ap()
    d_x1 = nc.dram_tensor("x1sc", [T, D], F32).ap()
    d_agin_gu = nc.dram_tensor("agin_gu", [GUT_SL, 128, 2048], BF16).ap()
    d_agout_gu = nc.dram_tensor("agout_gu", [GUT, 128, 2048], BF16, addr_space="Shared").ap()
    d_agin_dn = nc.dram_tensor("agin_dn", [2, 128, NIT, 128], BF16).ap()
    if debug:
        d_dbg_x1 = nc.dram_tensor("dbg_x1", [T, D], F32, kind="ExternalOutput").ap()
        d_dbg_gu = nc.dram_tensor("dbg_gu", [GUT_SL, 128, 2048], BF16, kind="ExternalOutput").ap()
        d_dbg_dn = nc.dram_tensor("dbg_dn", [2, 128, NIT, 128], BF16, kind="ExternalOutput").ap()
        d_dbg_wbf = nc.dram_tensor("dbg_wbf", [D, 5120], BF16, kind="ExternalOutput").ap()
        d_dbg_act = nc.dram_tensor("dbg_act", [10, 128, 8320], BF16, kind="ExternalOutput").ap()
    d_agout_dn = nc.dram_tensor("agout_dn", [NCORE, 2, 128, NIT, 128], BF16, addr_space="Shared").ap()

    P = Prog()
    import os as _os
    P.SEQ = bool(_os.environ.get("BASSSEQ"))
    P.PEDRAIN = bool(_os.environ.get("BASSPD", "1")) and not _os.environ.get("BASSNOPD")
    P.SEQA = bool(_os.environ.get("BASSSEQA"))
    P.SEQF = bool(_os.environ.get("BASSSEQF"))
    P.SEQTAG = _os.environ.get("BASSSEQT", "")
    P.ALLDRAIN = bool(_os.environ.get("BASSAD"))
    es = ExitStack()

    def sb(name, shape, dt):
        return es.enter_context(nc.sbuf_tensor(name, shape, dt))

    s_consts_cos = sb("c_cos", [128, NM * 64], F32)
    s_consts_sin = sb("c_sin", [128, NM * 64], F32)
    s_logp = sb("c_logp", [128, NM], F32)
    s_scaler = sb("c_scaler", [128, H], F32)
    s_gnw = sb("c_gnw", [128, 128], F32)
    s_ident = sb("c_ident", [128, 128], BF16)
    s_identf = sb("c_identf", [128, 128], F32)
    s_ones1 = sb("c_ones1", [1, 128], F32)
    s_mTL = sb("c_mTL", [128, 128], F32)
    s_mTR = sb("c_mTR", [128, 128], F32)
    s_cb = sb("c_cb", [128, 4], F32)

    s_lnT = sb("lnT", [128, 16 * 512], BF16)
    s_qT = sb("qT", [128, H * 512], BF16)
    s_kT = sb("kT", [128, KVH * T], BF16)
    s_vaug = sb("vaug", [128, NM * 520], BF16)
    s_sg = sb("sg", [128, QM * D], BF16)
    s_xt = [sb(f"xt{i}", [128, D], F32) for i in range(2)]
    s_wt = [sb(f"wt{i}", [128, 512], BF16) for i in range(4)]
    s_qraw = [sb(f"qraw{i}", [128, 512], F32) for i in range(2)]
    s_rope = sb("rope", [128, 1024], F32)
    s_ropeo = sb("ropeo", [128, 512], F32)
    s_qpost = sb("qpost", [128, 512], BF16)
    s_ms = sb("ms", [128, 64], F32)
    s_r4 = sb("r4", [128, 16], F32)
    s_stat = [sb(f"stat{i}", [128, 8], F32) for i in range(2)]
    s_pT = [sb(f"pT{i}", [128, 9 * 128], BF16) for i in range(2)]
    s_ym = [sb(f"ym{i}", [128, D], F32) for i in range(2)]
    s_x1st = [sb(f"x1st{i}", [128, D], F32) for i in range(2)]
    s_dump = sb("dump", [128, D], BF16)
    s_lnst = sb("lnst", [128, 16], F32)
    s_cast32 = [sb(f"cast32_{i}", [128, 512], F32) for i in range(2)]
    s_castbf = [sb(f"castbf{i}", [128, 512], BF16) for i in range(2)]
    s_dqp = [sb(f"dqp{i}", [128, 704], I32) for i in range(1)]
    s_dqs = sb("dqs", [128, NIT], F32)
    s_dqs16 = sb("dqs16", [128, NIT], F16)
    s_dqv = sb("dqv", [128, 1408], F32)
    s_dqt = [sb(f"dqt{i}", [128, 256], I32) for i in range(3)]
    s_dqf = [sb(f"dqf{i}", [128, 256], F32) for i in range(2)]
    s_dqb = sb("dqb", [128, 2048], BF16)
    s_stg = sb("stg", [128, 2048], BF16)
    s_stgd = sb("stgd", [128, 1408], BF16)

    ps = es.enter_context(nc.psum_tensor("ps", [128, 4096], F32))

    b_consts = Buf("consts")
    b_wbf = {w: Buf(f"wbf_{w}") for w in "qkvg"}
    b_x1hbm = Buf("x1hbm")
    b_agin_gu = Buf("agin_gu")
    b_agin_dn = Buf("agin_dn")
    b_agout = Buf("agout")
    b_lnT = Buf("lnT")
    b_qT = Buf("qT")
    b_kT = Buf("kT")
    b_vaug = Buf("vaug")
    b_sg = Buf("sg")
    b_xt = [Buf(f"xt{i}") for i in range(2)]
    b_xe = b_xt
    b_wt = [Buf(f"wt{i}") for i in range(4)]
    b_qraw = [Buf(f"qraw{i}") for i in range(2)]
    b_rope = Buf("rope")
    b_ropeo = Buf("ropeo")
    b_qpost = Buf("qpost")
    b_ms = Buf("ms")
    b_r4 = Buf("r4")
    b_stat = [Buf(f"stat{i}") for i in range(2)]
    b_pT = [Buf(f"pT{i}") for i in range(2)]
    b_ym = [Buf(f"ym{i}") for i in range(2)]
    b_x1st = [Buf(f"x1st{i}") for i in range(2)]
    b_dump = Buf("dump")
    b_lnst = Buf("lnst")
    b_cast32 = [Buf(f"cast32_{i}") for i in range(2)]
    b_castbf = [Buf(f"castbf{i}") for i in range(2)]
    b_dq = Buf("dq")
    b_stg = Buf("stg")
    b_stgd = Buf("stgd")
    b_bank = [Buf(f"bank{i}", excl=True) for i in range(8)]
    b_outhbm = Buf("outhbm")

    def banks(lo, hi):
        return [b_bank[i] for i in range(lo, hi + 1)]

    E = P.emit

    def ld(dst, src):
        E("gp", lambda g, dst=dst, src=src: g.dma_start(out=dst, in_=src),
          writes=[b_consts], kind="dma")

    ld(s_consts_cos[:], d_cos)
    ld(s_consts_sin[:], d_sin)
    ld(s_logp[:], d_logp)
    ld(s_scaler[:], d_scaler)
    ld(s_gnw[:], d_gnw)
    ld(s_ident[:], d_ident)
    ld(s_identf[:], d_identf)
    ld(s_ones1[:], d_onesr)
    ld(s_mTL[:], d_mTL)
    ld(s_mTR[:], d_mTR)
    E("dve", lambda v: v.memset(s_cb[:, 0:1], -0.5), writes=[b_consts])
    E("dve", lambda v: v.memset(s_vaug[:], 1.0), writes=[b_vaug])

    wsrc = [("q", d_wq, 2048, 0), ("k", d_wk, 512, 2048), ("v", d_wv, 512, 2560), ("g", d_wg, 2048, 3072)]
    slot_c = 0
    for wname, dsrc, wcols, coff in wsrc:
        for dc in range(16):
            for cc in range(0, wcols, 512):
                sl = slot_c % 2
                slot_c += 1
                E("sp", lambda s, dsrc=dsrc, dc=dc, sl=sl, cc=cc:
                  s.dma_start(out=s_cast32[sl][:], in_=dsrc[dc * 128:(dc + 1) * 128, cc:cc + 512]),
                  writes=[b_cast32[sl]], kind="dma")
                E("dve", lambda v, sl=sl: v.tensor_copy(out=s_castbf[sl][:], in_=s_cast32[sl][:]),
                  reads=[b_cast32[sl]], writes=[b_castbf[sl]])
                E("gp", lambda g, dc=dc, sl=sl, cc=cc, coff=coff:
                  g.dma_start(out=d_wbf[dc * 128:(dc + 1) * 128, coff + cc:coff + cc + 512], in_=s_castbf[sl][:]),
                  reads=[b_castbf[sl]], writes=[b_wbf[wname]], kind="dma")

    def emit_ln_quarter(q, src_hbm, b_srchbm, eps):
        for mloc in range(QM):
            m = q * QM + mloc
            sl = mloc % 2
            E("sp", lambda s, m=m, sl=sl, src_hbm=src_hbm:
              s.dma_start(out=s_xt[sl][:], in_=src_hbm[m * 128:(m + 1) * 128, :]),
              reads=([b_srchbm] if b_srchbm else []), writes=[b_xt[sl]], kind="dma")
            E("dve", lambda v, sl=sl, mloc=mloc:
              v.tensor_reduce(out=s_lnst[:, mloc:mloc + 1], in_=s_xt[sl][:], axis=AX, op=ALU.add),
              reads=[b_xt[sl]], writes=[b_lnst])
            E("act", lambda a, sl=sl, mloc=mloc:
              a.activation(out=s_dump[:], in_=s_xt[sl][:], func=AF.Square,
                           accum_out=s_lnst[:, 4 + mloc:5 + mloc]),
              reads=[b_xt[sl]], writes=[b_dump, b_lnst])
            E("dve", lambda v, mloc=mloc:
              v.tensor_scalar(out=s_lnst[:, 8 + mloc:9 + mloc], in0=s_lnst[:, mloc:mloc + 1],
                              scalar1=1.0 / D, scalar2=None, op0=ALU.mult),
              reads=[b_lnst], writes=[b_lnst])
            E("dve", lambda v, mloc=mloc:
              v.scalar_tensor_tensor(out=s_lnst[:, 12 + mloc:13 + mloc], in0=s_lnst[:, 8 + mloc:9 + mloc],
                                     scalar=-1.0, in1=s_lnst[:, 8 + mloc:9 + mloc], op0=ALU.mult, op1=ALU.mult),
              reads=[b_lnst], writes=[b_lnst])
            E("dve", lambda v, mloc=mloc, eps=eps:
              v.scalar_tensor_tensor(out=s_lnst[:, 12 + mloc:13 + mloc], in0=s_lnst[:, 4 + mloc:5 + mloc],
                                     scalar=1.0 / D, in1=s_lnst[:, 12 + mloc:13 + mloc], op0=ALU.mult, op1=ALU.add),
              reads=[b_lnst], writes=[b_lnst])
            E("dve", lambda v, mloc=mloc, eps=eps:
              v.tensor_scalar(out=s_lnst[:, 12 + mloc:13 + mloc], in0=s_lnst[:, 12 + mloc:13 + mloc],
                              scalar1=1.0, scalar2=eps, op0=ALU.mult, op1=ALU.add),
              reads=[b_lnst], writes=[b_lnst])
            E("act", lambda a, mloc=mloc: a.activation(out=s_lnst[:, 12 + mloc:13 + mloc], in_=s_lnst[:, 12 + mloc:13 + mloc], func=AF.Ln),
              reads=[b_lnst], writes=[b_lnst])
            E("act", lambda a, mloc=mloc: a.activation(out=s_lnst[:, 12 + mloc:13 + mloc], in_=s_lnst[:, 12 + mloc:13 + mloc], func=AF.Exp,
                                        scale=s_cb[:, 0:1]),
              reads=[b_lnst, b_consts], writes=[b_lnst])
            E("dve", lambda v, mloc=mloc:
              v.scalar_tensor_tensor(out=s_lnst[:, 8 + mloc:9 + mloc], in0=s_lnst[:, 8 + mloc:9 + mloc],
                                     scalar=-1.0, in1=s_lnst[:, 12 + mloc:13 + mloc], op0=ALU.mult, op1=ALU.mult),
              reads=[b_lnst], writes=[b_lnst])
            E("act", lambda a, sl=sl, mloc=mloc:
              a.activation(out=s_dump[:], in_=s_xt[sl][:], func=AF.Identity,
                           bias=s_lnst[:, 8 + mloc:9 + mloc], scale=s_lnst[:, 12 + mloc:13 + mloc]),
              reads=[b_xt[sl], b_lnst], writes=[b_dump])
            pbf = ps[:, 0:1024].bitcast(BF16)
            for dc in range(16):
                E("pe", lambda t, dc=dc, pbf=pbf:
                  t.transpose(pbf[:, dc * 128:(dc + 1) * 128], s_dump[:, dc * 128:(dc + 1) * 128], s_ident[:]),
                  reads=[b_dump, b_consts] + banks(0, 1), writes=[])
            dst = s_lnT[:, :].rearrange("p (dc t) -> p dc t", dc=16)[:, :, mloc * 128:(mloc + 1) * 128]
            E("act", lambda a, dst=dst, pbf=pbf:
              a.activation(out=dst, in_=pbf.rearrange("p (dc t) -> p dc t", dc=16), func=AF.Copy),
              reads=banks(0, 1), writes=[b_lnT])

    def emit_proj_quarter(q):
        jobs = []
        for oc in range(4):
            jobs.append(("q", oc, oc * 512))
        jobs.append(("k", 0, 2048))
        jobs.append(("v", 0, 2560))
        for oc in range(4):
            jobs.append(("g", oc, 3072 + oc * 512))
        slot = 0
        for wname, oc, coff in jobs:
            for dc in range(16):
                sl = slot % 4
                slot += 1
                E("sp", lambda s, dc=dc, coff=coff, sl=sl:
                  s.dma_start(out=s_wt[sl][:], in_=d_wbf[dc * 128:(dc + 1) * 128, coff:coff + 512]),
                  reads=[b_wbf[wname]], writes=[b_wt[sl]], kind="dma")
                for mloc in range(QM):
                    lhsT = s_lnT[:, dc * 512 + mloc * 128: dc * 512 + (mloc + 1) * 128]
                    pm = ps[:, 1024 + mloc * 512: 1024 + (mloc + 1) * 512]
                    E("pe", lambda t, lhsT=lhsT, pm=pm, sl=sl, dc=dc:
                      t.matmul(pm, lhsT, s_wt[sl][:], start=(dc == 0), stop=(dc == 15),
                               skip_group_check=True),
                      reads=[b_lnT, b_wt[sl]] + [b_bank[2 + mloc]], writes=[])
            for mloc in range(QM):
                m = q * QM + mloc
                pm = ps[:, 1024 + mloc * 512: 1024 + (mloc + 1) * 512]
                if wname == "q" or wname == "k":
                    emit_qk_post(q, m, mloc, oc, wname, pm)
                elif wname == "v":
                    dst = s_vaug[:, m * 520: m * 520 + 520].rearrange("p (kv c) -> p kv c", kv=4)[:, :, 0:128]
                    E("act", lambda a, dst=dst, pm=pm:
                      a.activation(out=dst, in_=pm.rearrange("p (kv c) -> p kv c", kv=4), func=AF.Copy),
                      reads=[b_bank[2 + mloc]], writes=[b_vaug])
                else:
                    E("act", lambda a, pm=pm, mloc=mloc, oc=oc:
                      a.activation(out=s_sg[:, mloc * D + oc * 512: mloc * D + (oc + 1) * 512],
                                   in_=pm, func=AF.Silu),
                      reads=[b_bank[2 + mloc]], writes=[b_sg])

    def emit_qk_post(q, m, mloc, oc, wname, pm):
        _oldtag = P.tag
        P.tag = "p1c"
        sl = m % 2
        E("act", lambda a, pm=pm, sl=sl: a.activation(out=s_qraw[sl][:], in_=pm, func=AF.Copy),
          reads=[pm_bank(mloc)], writes=[b_qraw[sl]])
        for hh in range(4):
            E("dve", lambda v, sl=sl, hh=hh:
              v.scalar_tensor_tensor(out=s_dump[:].bitcast(F32)[:, 0:128], in0=s_qraw[sl][:, hh * 128:(hh + 1) * 128],
                                     scalar=1.0, in1=s_qraw[sl][:, hh * 128:(hh + 1) * 128],
                                     op0=ALU.mult, op1=ALU.mult, accum_out=s_ms[:, hh:hh + 1]),
              reads=[b_qraw[sl]], writes=[b_dump, b_ms])
        E("dve", lambda v: v.tensor_scalar(out=s_ms[:, 0:4], in0=s_ms[:, 0:4], scalar1=1.0 / DH,
                                           scalar2=EPS_RMS, op0=ALU.mult, op1=ALU.add),
          reads=[b_ms], writes=[b_ms])
        E("act", lambda a: a.activation(out=s_r4[:, 0:4], in_=s_ms[:, 0:4], func=AF.Ln),
          reads=[b_ms], writes=[b_r4])
        E("act", lambda a: a.activation(out=s_r4[:, 0:4], in_=s_r4[:, 0:4], func=AF.Exp, scale=s_cb[:, 0:1]),
          reads=[b_r4, b_consts], writes=[b_r4])
        if wname == "q":
            E("dve", lambda v, oc=oc:
              v.tensor_tensor(out=s_r4[:, 4:8], in0=s_r4[:, 0:4], in1=s_scaler[:, oc * 4:(oc + 1) * 4], op=ALU.mult),
              reads=[b_r4, b_consts], writes=[b_r4])
            E("dve", lambda v, m=m:
              v.tensor_scalar(out=s_r4[:, 4:8], in0=s_r4[:, 4:8], scalar1=s_logp[:, m:m + 1],
                              scalar2=1.0 / math.sqrt(DH), op0=ALU.mult, op1=ALU.mult),
              reads=[b_r4, b_consts], writes=[b_r4])
            rcol = 4
        else:
            rcol = 0
        qv = s_qraw[sl][:, 0:512].rearrange("p (h twod) -> p h twod", h=4)
        ov = s_ropeo[:, 0:512].rearrange("p (h twod) -> p h twod", h=4)
        X1, X2 = qv[:, :, 0:64], qv[:, :, 64:128]
        Y1, Y2 = ov[:, :, 0:64], ov[:, :, 64:128]
        C = s_consts_cos[:, m * 64:(m + 1) * 64].rearrange("p (one d) -> p one d", one=1).broadcast_to([128, 4, 64])
        S = s_consts_sin[:, m * 64:(m + 1) * 64].rearrange("p (one d) -> p one d", one=1).broadcast_to([128, 4, 64])
        rA = s_rope[:, 0:256].rearrange("p (h d) -> p h d", h=4)
        rB = s_rope[:, 256:512].rearrange("p (h d) -> p h d", h=4)
        rD = s_rope[:, 512:768].rearrange("p (h d) -> p h d", h=4)
        rE = s_rope[:, 768:1024].rearrange("p (h d) -> p h d", h=4)
        E("dve", lambda v, X1=X1, C=C, rA=rA: v.tensor_tensor(out=rA, in0=X1, in1=C, op=ALU.mult),
          reads=[b_qraw[sl], b_consts], writes=[b_rope])
        E("dve", lambda v, X2=X2, S=S, rB=rB: v.tensor_tensor(out=rB, in0=X2, in1=S, op=ALU.mult),
          reads=[b_qraw[sl], b_consts], writes=[b_rope])
        E("dve", lambda v, X1=X1, S=S, rD=rD: v.tensor_tensor(out=rD, in0=X1, in1=S, op=ALU.mult),
          reads=[b_qraw[sl], b_consts], writes=[b_rope])
        E("dve", lambda v, X2=X2, C=C, rE=rE: v.tensor_tensor(out=rE, in0=X2, in1=C, op=ALU.mult),
          reads=[b_qraw[sl], b_consts], writes=[b_rope])
        E("dve", lambda v, Y1=Y1, rA=rA, rB=rB: v.tensor_tensor(out=Y1, in0=rA, in1=rB, op=ALU.add),
          reads=[b_rope], writes=[b_ropeo])
        E("dve", lambda v, Y2=Y2, rE=rE, rD=rD: v.tensor_tensor(out=Y2, in0=rE, in1=rD, op=ALU.subtract),
          reads=[b_rope], writes=[b_ropeo])
        for hh in range(4):
            E("dve", lambda v, hh=hh, rcol=rcol:
              v.tensor_scalar(out=s_qpost[:, hh * 128:(hh + 1) * 128], in0=s_ropeo[:, hh * 128:(hh + 1) * 128],
                              scalar1=s_r4[:, rcol + hh:rcol + hh + 1], scalar2=None, op0=ALU.mult),
              reads=[b_ropeo, b_r4], writes=[b_qpost])
        P.tag = "p1t"
        pbf6 = ps[:, 3072:3584].bitcast(BF16)
        for hh in range(4):
            slot6 = hh % 8
            E("pe", lambda t, hh=hh, slot6=slot6, pbf6=pbf6:
              t.transpose(pbf6[:, slot6 * 128:(slot6 + 1) * 128], s_qpost[:, hh * 128:(hh + 1) * 128], s_ident[:]),
              reads=[b_qpost, b_consts, b_bank[6]], writes=[])
            if wname == "q":
                h = oc * 4 + hh
                dst = s_qT[:, h * 512 + mloc * 128: h * 512 + (mloc + 1) * 128]
                bdst = b_qT
            else:
                kv = hh
                dst = s_kT[:, kv * T + m * 128: kv * T + (m + 1) * 128]
                bdst = b_kT
            E("act", lambda a, dst=dst, slot6=slot6, pbf6=pbf6:
              a.activation(out=dst, in_=pbf6[:, slot6 * 128:(slot6 + 1) * 128], func=AF.Copy),
              reads=[b_bank[6]], writes=[bdst])
        P.tag = _oldtag

    def pm_bank(mloc):
        return b_bank[2 + mloc]

    def emit_attention_quarter(q):
        units = []
        for mloc in range(QM):
            for h in range(H):
                units.append((q * QM + mloc, h))
        NU = len(units)

        def scores(u, m, h):
            par = u % 2
            kh = h // 4
            nj = (m + 1) if m < 8 else 9
            kt0 = max(0, m - 8)
            base = 0 if par == 0 else 1536
            stb = banks(0, 2) if par == 0 else banks(3, 5)
            for j in range(nj):
                kt = kt0 + j
                lhsT = s_kT[:, kh * T + kt * 128: kh * T + (kt + 1) * 128]
                rhs = s_qT[:, h * 512 + (m % QM) * 128: h * 512 + (m % QM + 1) * 128]
                outp = ps[:, base + j * 128: base + (j + 1) * 128]
                E("pe", lambda t, lhsT=lhsT, rhs=rhs, outp=outp:
                  t.matmul(outp, lhsT, rhs, start=True, stop=True),
                  reads=[b_kT, b_qT] + stb, writes=[])
            return par, nj, base, stb

        def softmax_pv(u, m, h, par, nj, base, stb):
            kh = h // 4
            W_ = nj * 128
            if m >= 8:
                E("dve", lambda v, base=base:
                  v.tensor_tensor(out=ps[:, base:base + 128], in0=ps[:, base:base + 128], in1=s_mTL[:], op=ALU.add),
                  reads=[b_consts] + stb, writes=[])
            E("dve", lambda v, base=base, nj=nj:
              v.tensor_tensor(out=ps[:, base + (nj - 1) * 128: base + nj * 128],
                              in0=ps[:, base + (nj - 1) * 128: base + nj * 128], in1=s_mTR[:], op=ALU.add),
              reads=[b_consts] + stb, writes=[])
            ybank0 = 6 + (u % 2)
            yb0 = ybank0 * 512
            E("dve", lambda v, base=base, W_=W_, par=par:
              v.tensor_reduce(out=s_stat[par][:, 0:1], in_=ps[:, base:base + W_], axis=AX, op=ALU.max),
              reads=stb, writes=[b_stat[par]])
            E("pe", lambda t, par=par, yb0=yb0:
              t.matmul(ps[0:1, yb0 + 256:yb0 + 384], s_stat[par][:, 0:1], s_identf[:], start=True, stop=True),
              reads=[b_stat[par], b_consts, b_bank[ybank0]], writes=[])
            E("dve", lambda v, par=par, yb0=yb0:
              v.tensor_reduce(out=s_stat[par][0:1, 3:4], in_=ps[0:1, yb0 + 256:yb0 + 384], axis=AX, op=ALU.max),
              reads=[b_bank[ybank0]], writes=[b_stat[par]])
            E("dve", lambda v, par=par:
              v.tensor_scalar(out=s_stat[par][0:1, 4:5], in0=s_stat[par][0:1, 3:4], scalar1=-1.0, scalar2=None, op0=ALU.mult),
              reads=[b_stat[par]], writes=[b_stat[par]])
            E("pe", lambda t, par=par, yb0=yb0:
              t.matmul(ps[:, yb0 + 384:yb0 + 385], s_ones1[:], s_stat[par][0:1, 4:5], start=True, stop=True),
              reads=[b_stat[par], b_consts, b_bank[ybank0]], writes=[])
            E("dve", lambda v, par=par, yb0=yb0:
              v.tensor_copy(out=s_stat[par][:, 2:3], in_=ps[:, yb0 + 384:yb0 + 385]),
              reads=[b_bank[ybank0]], writes=[b_stat[par]])
            E("act", lambda a, base=base, W_=W_, par=par:
              a.activation(out=s_pT[par][:, 0:W_], in_=ps[:, base:base + W_], func=AF.Exp,
                           bias=s_stat[par][:, 2:3]),
              reads=[b_stat[par]] + stb, writes=[b_pT[par]])
            ybank = 6 + (u % 2)
            ybase = ybank * 512
            for j in range(nj):
                kt = max(0, m - 8) + j
                rhs = s_vaug[:, kt * 520 + kh * 130: kt * 520 + kh * 130 + 129]
                E("pe", lambda t, j=j, rhs=rhs, par=par, ybase=ybase, nj=nj, ybank=ybank:
                  t.matmul(ps[:, ybase:ybase + 129], s_pT[par][:, j * 128:(j + 1) * 128], rhs,
                           start=(j == 0), stop=(j == nj - 1)),
                  reads=[b_pT[par], b_vaug, b_bank[ybank]], writes=[])
            mpar = m % 2
            E("dve", lambda v, ybase=ybase, par=par, ybank=ybank:
              v.reciprocal(out=s_stat[par][:, 3:4], in_=ps[:, ybase + 128:ybase + 129]),
              reads=[b_bank[ybank]], writes=[b_stat[par]])
            E("dve", lambda v, ybase=ybase, par=par, h=h, mpar=mpar, ybank=ybank:
              v.tensor_scalar(out=s_ym[mpar][:, h * 128:(h + 1) * 128], in0=ps[:, ybase:ybase + 128],
                              scalar1=s_stat[par][:, 3:4], scalar2=None, op0=ALU.mult),
              reads=[b_bank[ybank], b_stat[par]], writes=[b_ym[mpar]])

        pending = None
        for u in range(NU + 1):
            if u < NU:
                m, h = units[u]
                info = scores(u, m, h)
            if pending is not None:
                softmax_pv(*pending)
            if u < NU:
                pending = (u, units[u][0], units[u][1]) + info
                if units[u][1] == 0:
                    m = units[u][0]
                    sl = m % 2
                    E("sp", lambda s, m=m, sl=sl:
                      s.dma_start(out=s_xt[sl][:], in_=d_x[m * 128:(m + 1) * 128, :]),
                      writes=[b_xe[sl]], kind="dma")
            else:
                pending = None
            if u >= 1 and units[u - 1][1] == H - 1:
                emit_attn_epilogue(units[u - 1][0], q)

    def emit_attn_epilogue(m, q):
        mloc = m % QM
        mpar = m % 2
        E("dve", lambda v, mpar=mpar, mloc=mloc:
          v.tensor_tensor(out=s_ym[mpar][:], in0=s_ym[mpar][:], in1=s_sg[:, mloc * D:(mloc + 1) * D], op=ALU.mult),
          reads=[b_ym[mpar], b_sg], writes=[b_ym[mpar]])
        for h in range(H):
            E("dve", lambda v, mpar=mpar, h=h:
              v.scalar_tensor_tensor(out=s_dump[:].bitcast(F32)[:, 0:128], in0=s_ym[mpar][:, h * 128:(h + 1) * 128],
                                     scalar=1.0, in1=s_ym[mpar][:, h * 128:(h + 1) * 128],
                                     op0=ALU.mult, op1=ALU.mult, accum_out=s_ms[:, 16 + h:17 + h]),
              reads=[b_ym[mpar]], writes=[b_dump, b_ms])
        E("dve", lambda v: v.tensor_scalar(out=s_ms[:, 16:32], in0=s_ms[:, 16:32], scalar1=1.0 / DH,
                                           scalar2=EPS_RMS, op0=ALU.mult, op1=ALU.add),
          reads=[b_ms], writes=[b_ms])
        E("act", lambda a: a.activation(out=s_ms[:, 32:48], in_=s_ms[:, 16:32], func=AF.Ln),
          reads=[b_ms], writes=[b_ms])
        E("act", lambda a: a.activation(out=s_ms[:, 32:48], in_=s_ms[:, 32:48], func=AF.Exp, scale=s_cb[:, 0:1]),
          reads=[b_ms, b_consts], writes=[b_ms])
        for h in range(H):
            E("dve", lambda v, mpar=mpar, h=h:
              v.tensor_scalar(out=s_x1st[mpar][:, h * 128:(h + 1) * 128], in0=s_ym[mpar][:, h * 128:(h + 1) * 128],
                              scalar1=s_ms[:, 32 + h:33 + h], scalar2=None, op0=ALU.mult),
              reads=[b_ym[mpar], b_ms], writes=[b_x1st[mpar]])
        E("dve", lambda v, mpar=mpar:
          v.tensor_tensor(out=s_x1st[mpar][:].rearrange("p (h d) -> p h d", h=16),
                          in0=s_x1st[mpar][:].rearrange("p (h d) -> p h d", h=16),
                          in1=s_gnw[:, :].rearrange("p (one d) -> p one d", one=1).broadcast_to([128, 16, 128]), op=ALU.mult),
          reads=[b_x1st[mpar], b_consts], writes=[b_x1st[mpar]])
        E("dve", lambda v, mpar=mpar:
          v.tensor_tensor(out=s_x1st[mpar][:], in0=s_x1st[mpar][:], in1=s_xt[mpar][:], op=ALU.add),
          reads=[b_x1st[mpar], b_xe[mpar]], writes=[b_x1st[mpar]])
        E("gp", lambda g, m=m, mpar=mpar:
          g.dma_start(out=d_x1[m * 128:(m + 1) * 128, :], in_=s_x1st[mpar][:]),
          reads=[b_x1st[mpar]], writes=[b_x1hbm], kind="dma")

    def emit_dequant_gu_tile(tl):
        E("sp", lambda s, tl=tl:
          s.dma_start(out=s_dqp[0][:, 0:256], in_=d_gup[tl * 128:(tl + 1) * 128, :]),
          writes=[b_dq], kind="dma")
        E("sp", lambda s, tl=tl:
          s.dma_start(out=s_dqs16[:, 0:16], in_=d_gus[tl * 128:(tl + 1) * 128, :]),
          writes=[b_dq], kind="dma")
        E("dve", lambda v: v.tensor_copy(out=s_dqs[:, 0:16], in_=s_dqs16[:, 0:16]),
          reads=[b_dq], writes=[b_dq])
        for half in range(2):
            emit_dequant_core(s_dqp[0][:, half * 128:(half + 1) * 128], 128, s_dqv, 1024)
            for gl in range(8):
                g = half * 8 + gl
                E("dve", lambda v, g=g, gl=gl:
                  v.tensor_scalar(out=s_dqb[:, g * 128:(g + 1) * 128], in0=s_dqv[:, gl * 128:(gl + 1) * 128],
                                  scalar1=s_dqs[:, g:g + 1], scalar2=None, op0=ALU.mult),
                  reads=[b_dq], writes=[b_dq])
        pbf7 = ps[:, 3584 + 0:4096].bitcast(BF16)
        for dc in range(16):
            slot = dc % 8
            E("pe", lambda t, dc=dc, slot=slot, pbf7=pbf7:
              t.transpose(pbf7[:, slot * 128:(slot + 1) * 128], s_dqb[:, dc * 128:(dc + 1) * 128], s_ident[:]),
              reads=[b_dq, b_consts, b_bank[7]], writes=[])
            E("act", lambda a, dc=dc, slot=slot, pbf7=pbf7:
              a.activation(out=s_stg[:, dc * 128:(dc + 1) * 128], in_=pbf7[:, slot * 128:(slot + 1) * 128], func=AF.Copy),
              reads=[b_bank[7]], writes=[b_stg])
        E("gp", lambda g, tl=tl:
          g.dma_start(out=d_agin_gu[tl, :, :], in_=s_stg[:]),
          reads=[b_stg], writes=[b_agin_gu], kind="dma")

    def emit_dequant_core(src_i32, ncols, dstf32, nd):
        t1, m3, s8 = s_dqt[0], s_dqt[1], s_dqt[2]
        mf, uu = s_dqf[0], s_dqf[1]
        for j in range(8):
            if j == 0:
                E("dve", lambda v, src_i32=src_i32, ncols=ncols:
                  v.tensor_copy(out=t1[:, 0:ncols], in_=src_i32[:, 0:ncols]),
                  reads=[b_dq], writes=[b_dq])
            else:
                E("dve", lambda v, src_i32=src_i32, ncols=ncols, j=j:
                  v.tensor_single_scalar(out=t1[:, 0:ncols], in_=src_i32[:, 0:ncols], scalar=4 * j,
                                         op=ALU.logical_shift_right),
                  reads=[b_dq], writes=[b_dq])
            E("dve", lambda v, ncols=ncols: v.tensor_single_scalar(out=m3[:, 0:ncols], in_=t1[:, 0:ncols], scalar=7, op=ALU.bitwise_and),
              reads=[b_dq], writes=[b_dq])
            E("dve", lambda v, ncols=ncols: v.tensor_single_scalar(out=s8[:, 0:ncols], in_=t1[:, 0:ncols], scalar=8, op=ALU.bitwise_and),
              reads=[b_dq], writes=[b_dq])
            E("dve", lambda v, ncols=ncols: v.tensor_copy(out=mf[:, 0:ncols], in_=m3[:, 0:ncols]),
              reads=[b_dq], writes=[b_dq])
            E("dve", lambda v, ncols=ncols: v.tensor_single_scalar(out=m3[:, 0:ncols], in_=m3[:, 0:ncols], scalar=22, op=ALU.logical_shift_left),
              reads=[b_dq], writes=[b_dq])
            E("dve", lambda v, ncols=ncols: v.tensor_single_scalar(out=m3[:, 0:ncols], in_=m3[:, 0:ncols], scalar=0x3F000000, op=ALU.add),
              reads=[b_dq], writes=[b_dq])
            E("dve", lambda v, ncols=ncols: v.tensor_scalar(out=uu[:, 0:ncols], in0=mf[:, 0:ncols], scalar1=-0.25, scalar2=0.5,
                                                            op0=ALU.mult, op1=ALU.add),
              reads=[b_dq], writes=[b_dq])
            E("dve", lambda v, ncols=ncols: v.tensor_single_scalar(out=uu[:, 0:ncols], in_=uu[:, 0:ncols], scalar=0.0, op=ALU.max),
              reads=[b_dq], writes=[b_dq])
            E("dve", lambda v, ncols=ncols: v.tensor_tensor(out=mf[:, 0:ncols], in0=m3[:, 0:ncols].bitcast(F32), in1=uu[:, 0:ncols], op=ALU.subtract),
              reads=[b_dq], writes=[b_dq])
            E("dve", lambda v, ncols=ncols: v.tensor_single_scalar(out=s8[:, 0:ncols], in_=s8[:, 0:ncols], scalar=28, op=ALU.logical_shift_left),
              reads=[b_dq], writes=[b_dq])
            E("dve", lambda v, ncols=ncols, j=j, dstf32=dstf32, nd=nd:
              v.tensor_tensor(out=dstf32[:, 0:nd].rearrange("p (c e) -> p c e", e=8)[:, :, j:j + 1].bitcast(I32),
                              in0=mf[:, 0:ncols].bitcast(I32).rearrange("p (c one) -> p c one", one=1),
                              in1=s8[:, 0:ncols].rearrange("p (c one) -> p c one", one=1), op=ALU.bitwise_or),
              reads=[b_dq], writes=[b_dq])

    def emit_dequant_dn():
        for ot in range(2):
            E("sp", lambda s, ot=ot:
              s.dma_start(out=s_dqp[0][:, 0:704], in_=d_dnp[ot * 128:(ot + 1) * 128, :]),
              writes=[b_dq], kind="dma")
            E("sp", lambda s, ot=ot:
              s.dma_start(out=s_dqs16[:, 0:NIT], in_=d_dns[ot * 128:(ot + 1) * 128, :]),
              writes=[b_dq], kind="dma")
            E("dve", lambda v: v.tensor_copy(out=s_dqs[:, 0:NIT], in_=s_dqs16[:, 0:NIT]),
              reads=[b_dq], writes=[b_dq])
            for half in range(4):
                sub = s_dqp[0][:, half * 176:(half + 1) * 176]
                emit_dequant_core(sub, 176, s_dqv, 1408)
                for gl in range(11):
                    g = half * 11 + gl
                    E("dve", lambda v, gl=gl, g=g:
                      v.tensor_scalar(out=s_dqb[:, gl * 128:(gl + 1) * 128], in0=s_dqv[:, gl * 128:(gl + 1) * 128],
                                      scalar1=s_dqs[:, g:g + 1], scalar2=None, op0=ALU.mult),
                      reads=[b_dq], writes=[b_dq])
                pbf7 = ps[:, 3584:4096].bitcast(BF16)
                for gl in range(11):
                    slot = gl % 8
                    E("pe", lambda t, gl=gl, slot=slot, pbf7=pbf7:
                      t.transpose(pbf7[:, slot * 128:(slot + 1) * 128], s_dqb[:, gl * 128:(gl + 1) * 128], s_ident[:]),
                      reads=[b_dq, b_consts, b_bank[7]], writes=[])
                    E("act", lambda a, gl=gl, slot=slot, pbf7=pbf7:
                      a.activation(out=s_stgd[:, gl * 128:(gl + 1) * 128],
                                   in_=pbf7[:, slot * 128:(slot + 1) * 128], func=AF.Copy),
                      reads=[b_bank[7]], writes=[b_stgd])
                it0 = half * 11
                E("gp", lambda g, ot=ot, it0=it0:
                  g.dma_start(out=d_agin_dn[ot, :, it0:it0 + 11, :],
                              in_=s_stgd[:].rearrange("p (it ds) -> p it ds", it=11)),
                  reads=[b_stgd], writes=[b_agin_dn], kind="dma")

    P.tag = "p1"
    emit_ln_quarter(0, d_x, None, EPS_LN)
    emit_proj_quarter(0)
    if debug:
        E("gp", lambda g: g.dma_start(out=d_dbg_act[0, :, 0:8192], in_=s_lnT[:]), reads=[b_lnT], writes=[], kind="dma")
        E("gp", lambda g: g.dma_start(out=d_dbg_act[1, :, 0:8192], in_=s_qT[:]), reads=[b_qT], writes=[], kind="dma")
        E("gp", lambda g: g.dma_start(out=d_dbg_act[2, :, 0:8192], in_=s_kT[:]), reads=[b_kT], writes=[], kind="dma")
        E("gp", lambda g: g.dma_start(out=d_dbg_act[3, :, 0:8320], in_=s_vaug[:]), reads=[b_vaug], writes=[], kind="dma")
        E("gp", lambda g: g.dma_start(out=d_dbg_act[4, :, 0:8192], in_=s_sg[:]), reads=[b_sg], writes=[], kind="dma")
    P.tag = "p2"
    emit_attention_quarter(0)
    P.tag = "dq"
    for tl in range(0, 4):
        emit_dequant_gu_tile(tl)
    P.tag = "p1"
    emit_ln_quarter(1, d_x, None, EPS_LN)
    emit_proj_quarter(1)
    P.tag = "p2"
    emit_attention_quarter(1)
    P.tag = "dq"
    for tl in range(4, 8):
        emit_dequant_gu_tile(tl)
    P.tag = "p1"
    emit_ln_quarter(2, d_x, None, EPS_LN)
    emit_proj_quarter(2)
    P.tag = "p2"
    emit_attention_quarter(2)
    P.tag = "dq"
    for tl in range(8, 11):
        emit_dequant_gu_tile(tl)
    emit_dequant_dn()
    E("gp", lambda g: g.collective_compute(
        "AllGather", ALU.bypass, replica_groups=[list(range(NCORE))],
        ins=[d_agin_gu.opt()], outs=[d_agout_gu.opt()]),
      reads=[b_agin_gu], writes=[b_agout], kind="coll")
    E("gp", lambda g: g.collective_compute(
        "AllGather", ALU.bypass, replica_groups=[list(range(NCORE))],
        ins=[d_agin_dn.opt()], outs=[d_agout_dn.opt()]),
      reads=[b_agin_dn], writes=[b_agout], kind="coll")
    P.tag = "p1"
    emit_ln_quarter(3, d_x, None, EPS_LN)
    emit_proj_quarter(3)
    if debug:
        E("gp", lambda g: g.dma_start(out=d_dbg_act[5, :, 0:8192], in_=s_lnT[:]), reads=[b_lnT], writes=[], kind="dma")
        E("gp", lambda g: g.dma_start(out=d_dbg_act[6, :, 0:8192], in_=s_qT[:]), reads=[b_qT], writes=[], kind="dma")
        E("gp", lambda g: g.dma_start(out=d_dbg_act[7, :, 0:8192], in_=s_kT[:]), reads=[b_kT], writes=[], kind="dma")
        E("gp", lambda g: g.dma_start(out=d_dbg_act[8, :, 0:8320], in_=s_vaug[:]), reads=[b_vaug], writes=[], kind="dma")
        E("gp", lambda g: g.dma_start(out=d_dbg_act[9, :, 0:8192], in_=s_sg[:]), reads=[b_sg], writes=[], kind="dma")
    P.tag = "p2"
    emit_attention_quarter(3)
    P.tag = "other"

    if debug:
        E("gp", lambda g: g.dma_start(out=d_dbg_gu, in_=d_agin_gu), reads=[b_agin_gu], writes=[], kind="dma")
        E("gp", lambda g: g.dma_start(out=d_dbg_dn, in_=d_agin_dn), reads=[b_agin_dn], writes=[], kind="dma")
        E("gp", lambda g: g.dma_start(out=d_dbg_wbf, in_=d_wbf), reads=[b_wbf["q"], b_wbf["k"], b_wbf["v"], b_wbf["g"]], writes=[], kind="dma")
        E("gp", lambda g: g.dma_start(out=d_dbg_x1, in_=d_x1), reads=[b_x1hbm], writes=[], kind="dma")

    P.in_ffn = True
    P.barrier()
    es.close()
    es2 = ExitStack()

    def sb2(name, shape, dt):
        return es2.enter_context(nc.sbuf_tensor(name, shape, dt))

    f_ln2T = sb2("ln2T", [128, 16 * FB], BF16)
    f_hT = sb2("hT", [128, NIT * FB], BF16)
    f_wgu = [sb2(f"wgu{i}", [128, 2048], BF16) for i in range(4)]
    f_stmp = sb2("stmp", [128, FB], F32)
    f_wdp = [sb2(f"wdp{i}", [128, NIT * 256], BF16) for i in range(2)]
    f_y = sb2("fy", [128, D], F32)
    f_x1 = [sb2(f"fx1_{i}", [128, D], F32) for i in range(2)]
    f_xt = [sb2(f"fxt{i}", [128, D], F32) for i in range(2)]
    f_dump = sb2("fdump", [128, D], BF16)
    f_lnst = sb2("flnst", [128, 16], F32)
    f_cb = sb2("fcb", [128, 4], F32)
    f_ident = sb2("fident", [128, 128], BF16)
    f_stat = sb2("fstat", [128, 8], F32)

    b_ln2T = Buf("ln2T")
    b_hT = Buf("hT")
    b_fwgu = [Buf(f"fwgu{i}") for i in range(4)]
    b_fstmp = Buf("fstmp")
    b_fwdp = [Buf(f"fwdp{i}") for i in range(2)]
    b_fy = Buf("fy")
    b_fx1 = [Buf(f"fx1_{i}") for i in range(2)]
    b_fxt = [Buf(f"fxt{i}") for i in range(2)]
    b_fdump = Buf("fdump")
    b_flnst = Buf("flnst")
    b_fconsts = Buf("fconsts")
    b_fstat = Buf("fstat")

    E("dve", lambda v: v.memset(f_cb[:, 0:1], -0.5), writes=[b_fconsts])
    E("gp", lambda g: g.dma_start(out=f_ident[:], in_=d_ident), writes=[b_fconsts], kind="dma")

    def emit_ln2_block(blk):
        for mloc in range(QM):
            m = blk * QM + mloc
            sl = mloc % 2
            E("sp", lambda s, m=m, sl=sl:
              s.dma_start(out=f_xt[sl][:], in_=d_x1[m * 128:(m + 1) * 128, :]),
              reads=[b_x1hbm], writes=[b_fxt[sl]], kind="dma")
            E("dve", lambda v, sl=sl, mloc=mloc:
              v.tensor_reduce(out=f_lnst[:, mloc:mloc + 1], in_=f_xt[sl][:], axis=AX, op=ALU.add),
              reads=[b_fxt[sl]], writes=[b_flnst])
            E("act", lambda a, sl=sl, mloc=mloc:
              a.activation(out=f_dump[:], in_=f_xt[sl][:], func=AF.Square,
                           accum_out=f_lnst[:, 4 + mloc:5 + mloc]),
              reads=[b_fxt[sl]], writes=[b_fdump, b_flnst])
            E("dve", lambda v, mloc=mloc:
              v.tensor_scalar(out=f_lnst[:, 8 + mloc:9 + mloc], in0=f_lnst[:, mloc:mloc + 1],
                              scalar1=1.0 / D, scalar2=None, op0=ALU.mult),
              reads=[b_flnst], writes=[b_flnst])
            E("dve", lambda v, mloc=mloc:
              v.scalar_tensor_tensor(out=f_lnst[:, 12 + mloc:13 + mloc], in0=f_lnst[:, 8 + mloc:9 + mloc],
                                     scalar=-1.0, in1=f_lnst[:, 8 + mloc:9 + mloc], op0=ALU.mult, op1=ALU.mult),
              reads=[b_flnst], writes=[b_flnst])
            E("dve", lambda v, mloc=mloc:
              v.scalar_tensor_tensor(out=f_lnst[:, 12 + mloc:13 + mloc], in0=f_lnst[:, 4 + mloc:5 + mloc],
                                     scalar=1.0 / D, in1=f_lnst[:, 12 + mloc:13 + mloc], op0=ALU.mult, op1=ALU.add),
              reads=[b_flnst], writes=[b_flnst])
            E("dve", lambda v, mloc=mloc:
              v.tensor_scalar(out=f_lnst[:, 12 + mloc:13 + mloc], in0=f_lnst[:, 12 + mloc:13 + mloc],
                              scalar1=1.0, scalar2=EPS_LN, op0=ALU.mult, op1=ALU.add),
              reads=[b_flnst], writes=[b_flnst])
            E("act", lambda a, mloc=mloc: a.activation(out=f_lnst[:, 12 + mloc:13 + mloc], in_=f_lnst[:, 12 + mloc:13 + mloc], func=AF.Ln),
              reads=[b_flnst], writes=[b_flnst])
            E("act", lambda a, mloc=mloc: a.activation(out=f_lnst[:, 12 + mloc:13 + mloc], in_=f_lnst[:, 12 + mloc:13 + mloc], func=AF.Exp, scale=f_cb[:, 0:1]),
              reads=[b_flnst, b_fconsts], writes=[b_flnst])
            E("dve", lambda v, mloc=mloc:
              v.scalar_tensor_tensor(out=f_lnst[:, 8 + mloc:9 + mloc], in0=f_lnst[:, 8 + mloc:9 + mloc],
                                     scalar=-1.0, in1=f_lnst[:, 12 + mloc:13 + mloc], op0=ALU.mult, op1=ALU.mult),
              reads=[b_flnst], writes=[b_flnst])

            E("act", lambda a, sl=sl, mloc=mloc:
              a.activation(out=f_dump[:], in_=f_xt[sl][:], func=AF.Identity,
                           bias=f_lnst[:, 8 + mloc:9 + mloc], scale=f_lnst[:, 12 + mloc:13 + mloc]),
              reads=[b_fxt[sl], b_flnst], writes=[b_fdump])
            pbf = ps[:, 0:1024].bitcast(BF16)
            for dc in range(16):
                E("pe", lambda t, dc=dc, pbf=pbf:
                  t.transpose(pbf[:, dc * 128:(dc + 1) * 128], f_dump[:, dc * 128:(dc + 1) * 128], f_ident[:]),
                  reads=[b_fdump, b_fconsts] + banks(0, 1), writes=[])
            dst = f_ln2T[:, :].rearrange("p (dc t) -> p dc t", dc=16)[:, :, mloc * 128:(mloc + 1) * 128]
            E("act", lambda a, dst=dst, pbf=pbf:
              a.activation(out=dst, in_=pbf.rearrange("p (dc t) -> p dc t", dc=16), func=AF.Copy),
              reads=banks(0, 1), writes=[b_ln2T])

    def emit_gup_block(blk):
        for Tt in range(NIT):
            sg_ = Tt % 2
            su_ = 2 + Tt % 2
            Tu = Tt + NIT
            E("sp", lambda s, Tt=Tt, sg_=sg_:
              s.dma_start(out=f_wgu[sg_][:], in_=d_agout_gu[Tt, :, :]),
              reads=[b_agout], writes=[b_fwgu[sg_]], kind="dma")
            E("sp", lambda s, Tu=Tu, su_=su_:
              s.dma_start(out=f_wgu[su_][:], in_=d_agout_gu[Tu, :, :]),
              reads=[b_agout], writes=[b_fwgu[su_]], kind="dma")
            gb = 2 + (Tt % 2) * 2
            ub = gb + 1
            pg = ps[:, gb * 512:(gb + 1) * 512]
            pu = ps[:, ub * 512:(ub + 1) * 512]
            for dc in range(16):
                E("pe", lambda t, dc=dc, sg_=sg_, pg=pg:
                  t.matmul(pg, f_wgu[sg_][:, dc * 128:(dc + 1) * 128], f_ln2T[:, dc * FB:(dc + 1) * FB],
                           start=(dc == 0), stop=(dc == 15)),
                  reads=[b_fwgu[sg_], b_ln2T, b_bank[gb]], writes=[])
            for dc in range(16):
                E("pe", lambda t, dc=dc, su_=su_, pu=pu:
                  t.matmul(pu, f_wgu[su_][:, dc * 128:(dc + 1) * 128], f_ln2T[:, dc * FB:(dc + 1) * FB],
                           start=(dc == 0), stop=(dc == 15)),
                  reads=[b_fwgu[su_], b_ln2T, b_bank[ub]], writes=[])
            E("act", lambda a, pg=pg: a.activation(out=f_stmp[:], in_=pg, func=AF.Silu),
              reads=[b_bank[gb]], writes=[b_fstmp])
            E("dve", lambda v, pu=pu, Tt=Tt:
              v.tensor_tensor(out=f_hT[:, Tt * FB:(Tt + 1) * FB], in0=f_stmp[:], in1=pu, op=ALU.mult),
              reads=[b_fstmp, b_bank[ub]], writes=[b_hT])

    def emit_ffn_block(blk):
        emit_ln2_block(blk)
        emit_gup_block(blk)
        for dcq in range(8):
            wsl = dcq % 2
            for ot in range(2):
                dst = f_wdp[wsl][:, :].rearrange("p (it d) -> p it d", it=NIT)[:, :, ot * 128:(ot + 1) * 128]
                E("sp", lambda s, dcq=dcq, ot=ot, dst=dst:
                  s.dma_start(out=dst, in_=d_agout_dn[dcq, ot, :, :, :]),
                  reads=[b_agout], writes=[b_fwdp[wsl]], kind="dma")
            for tt in range(4):
                yb = 2 + tt
                py = ps[:, yb * 512: yb * 512 + 256]
                for it in range(NIT):
                    E("pe", lambda t, it=it, tt=tt, wsl=wsl, py=py:
                      t.matmul(py, f_hT[:, it * FB + tt * 128: it * FB + (tt + 1) * 128],
                               f_wdp[wsl][:, it * 256:(it + 1) * 256],
                               start=(it == 0), stop=(it == NIT - 1)),
                      reads=[b_hT, b_fwdp[wsl], b_bank[yb]], writes=[])
            for tt in range(4):
                yb = 2 + tt
                py = ps[:, yb * 512: yb * 512 + 256]
                tpar = tt % 2
                E("dve", lambda v, py=py, dcq=dcq, tt=tt, tpar=tpar:
                  v.tensor_copy(out=(f_x1[tpar] if tt < 2 else f_xt[tpar])[:, dcq * 256:(dcq + 1) * 256], in_=py),
                  reads=[b_bank[yb]], writes=[b_fx1[tpar] if tt < 2 else b_fxt[tpar]])
        for tt in range(4):
            m = blk * 4 + tt
            tpar = tt % 2
            ybuf = f_x1[tpar] if tt < 2 else f_xt[tpar]
            bybuf = b_fx1[tpar] if tt < 2 else b_fxt[tpar]
            E("dve", lambda v, ybuf=ybuf, bybuf=bybuf, tt=tt:
              v.scalar_tensor_tensor(out=f_dump[:, 0:D].bitcast(F32)[:, 0:1024], in0=ybuf[:, 0:1024], scalar=1.0,
                                     in1=ybuf[:, 0:1024], op0=ALU.mult, op1=ALU.mult,
                                     accum_out=f_stat[:, 0:1]),
              reads=[bybuf], writes=[b_fdump, b_fstat])
            E("dve", lambda v, ybuf=ybuf, bybuf=bybuf, tt=tt:
              v.scalar_tensor_tensor(out=f_dump[:, 0:D].bitcast(F32)[:, 0:1024], in0=ybuf[:, 1024:2048], scalar=1.0,
                                     in1=ybuf[:, 1024:2048], op0=ALU.mult, op1=ALU.mult,
                                     accum_out=f_stat[:, 1:2]),
              reads=[bybuf], writes=[b_fdump, b_fstat])
            E("dve", lambda v:
              v.tensor_tensor(out=f_stat[:, 2:3], in0=f_stat[:, 0:1], in1=f_stat[:, 1:2], op=ALU.add),
              reads=[b_fstat], writes=[b_fstat])
            E("dve", lambda v:
              v.tensor_scalar(out=f_stat[:, 2:3], in0=f_stat[:, 2:3], scalar1=1.0 / D, scalar2=EPS_RMS,
                              op0=ALU.mult, op1=ALU.add),
              reads=[b_fstat], writes=[b_fstat])
            E("act", lambda a: a.activation(out=f_stat[:, 3:4], in_=f_stat[:, 2:3], func=AF.Ln),
              reads=[b_fstat], writes=[b_fstat])
            E("act", lambda a: a.activation(out=f_stat[:, 3:4], in_=f_stat[:, 3:4], func=AF.Exp, scale=f_cb[:, 0:1]),
              reads=[b_fstat, b_fconsts], writes=[b_fstat])
            E("sp", lambda s, m=m:
              s.dma_start(out=f_y[:], in_=d_x1[m * 128:(m + 1) * 128, :]),
              reads=[b_x1hbm], writes=[b_fy], kind="dma")
            E("dve", lambda v, ybuf=ybuf, bybuf=bybuf:
              v.tensor_scalar(out=ybuf[:], in0=ybuf[:], scalar1=f_stat[:, 3:4], scalar2=None, op0=ALU.mult),
              reads=[bybuf, b_fstat], writes=[bybuf])
            E("dve", lambda v, ybuf=ybuf, bybuf=bybuf:
              v.tensor_tensor(out=ybuf[:], in0=ybuf[:], in1=f_y[:], op=ALU.add),
              reads=[bybuf, b_fy], writes=[bybuf])
            E("gp", lambda g, m=m, ybuf=ybuf, bybuf=bybuf:
              g.dma_start(out=d_out[m * 128:(m + 1) * 128, :], in_=ybuf[:]),
              reads=[bybuf], writes=[b_outhbm], kind="dma")

    for blk in range(NFB):
        emit_ffn_block(blk)

    sems = {}
    dma_sems = {}
    sem_stack = ExitStack()
    for e in Prog.ENGS:
        sems[e] = sem_stack.enter_context(nc.semaphore(f"sem_{e}"))
    for e in Prog.ENGS:
        dma_sems[e] = sem_stack.enter_context(nc.semaphore(f"dsem_{e}"))
    cc_sem = sem_stack.enter_context(nc.semaphore("ccsem"))

    with sem_stack, es2, nc.Block() as block:
        run_stream = P.resolve_and_replay(nc, None, sems, dma_sems, cc_sem)

        @block.tensor
        def _(tensor):
            run_stream("pe", tensor)

        @block.vector
        def _(vector):
            run_stream("dve", vector)

        @block.scalar
        def _(scalar):
            run_stream("act", scalar)

        @block.sync
        def _(sync):
            run_stream("sp", sync)
            sync.wait_ge(dma_sems["gp"], 16 * P.total_dma["gp"])
            sync.wait_ge(dma_sems["sp"], 16 * P.total_dma["sp"])

        @block.gpsimd
        def _(gpsimd):
            run_stream("gp", gpsimd)

    return nc


_NC_CACHE = None


def _host_constants():
    inv_freq = 1.0 / (10000.0 ** (np.arange(0, DH, 2, dtype=np.float64) / DH))
    t = np.arange(T, dtype=np.float64)
    fr = np.outer(t, inv_freq)
    cos = np.cos(fr).astype(np.float32)
    sin = np.sin(fr).astype(np.float32)
    cosb = np.zeros((128, NM * 64), np.float32)
    sinb = np.zeros((128, NM * 64), np.float32)
    for m in range(NM):
        cosb[:, m * 64:(m + 1) * 64] = cos[m * 128:(m + 1) * 128]
        sinb[:, m * 64:(m + 1) * 64] = sin[m * 128:(m + 1) * 128]
    logp = np.zeros((128, NM), np.float32)
    for m in range(NM):
        logp[:, m] = np.log(np.arange(m * 128 + 1, m * 128 + 129, dtype=np.float64)).astype(np.float32)
    ident = np.eye(128, dtype=np.float32).astype(ml_dtypes.bfloat16)
    identf = np.eye(128, dtype=np.float32)
    onesr = np.ones((1, 128), dtype=np.float32)
    p = np.arange(128)[:, None]
    qq = np.arange(128)[None, :]
    mTL = np.where(p >= qq, 0.0, NEG).astype(np.float32)
    mTR = np.where(p <= qq, 0.0, NEG).astype(np.float32)
    return cosb, sinb, logp, ident, identf, onesr, mTL, mTR


def kernel(*, _debug_hook=None, **_kw):
    return _kernel_impl(**_kw)


def _kernel_impl(L0_attn_hidden_states, L0_attn_Wq, L0_attn_Wk, L0_attn_Wv, L0_attn_Wg,
           L0_attn_softmax_scaler, L0_attn_gn_weight,
           L0_ffn_gate_up_packed, L0_ffn_gate_up_scales, L0_ffn_down_packed, L0_ffn_down_scales):
    import os
    global _NC_CACHE
    if _NC_CACHE is None:
        _NC_CACHE = build_nc(debug=bool(os.environ.get("BASSDBG")))
    nc = _NC_CACHE

    def _np(a, dt=None):
        a = np.asarray(a)
        if dt is not None and a.dtype != dt:
            a = a.astype(dt)
        return a

    x = _np(L0_attn_hidden_states, np.float32)
    wq = np.ascontiguousarray(_np(L0_attn_Wq, np.float32))
    wk = np.ascontiguousarray(_np(L0_attn_Wk, np.float32))
    wv = np.ascontiguousarray(_np(L0_attn_Wv, np.float32))
    wg = np.ascontiguousarray(_np(L0_attn_Wg, np.float32))
    scaler = _np(L0_attn_softmax_scaler, np.float32)
    gnw = _np(L0_attn_gn_weight, np.float32)
    gup = _np(L0_ffn_gate_up_packed, np.int32)
    gus = _np(L0_ffn_gate_up_scales)
    dnp = _np(L0_ffn_down_packed, np.int32)
    dns = _np(L0_ffn_down_scales)

    gus_t = np.ascontiguousarray(gus.T)
    dns_t = np.ascontiguousarray(dns.T)
    scaler_b = np.ascontiguousarray(np.broadcast_to(scaler[None, :], (128, H)))
    gnw_b = np.ascontiguousarray(np.broadcast_to(gnw[None, :], (128, 128)))
    cosb, sinb, logp, ident, identf, onesr, mTL, mTR = _host_constants()

    in_maps = []
    for c in range(NCORE):
        in_maps.append({
            "x": np.ascontiguousarray(x[c]),
            "wq": wq, "wk": wk, "wv": wv, "wg": wg,
            "gup_packed_sl": np.ascontiguousarray(gup[c * 1408:(c + 1) * 1408]),
            "gu_scales_sl": np.ascontiguousarray(gus_t[c * 1408:(c + 1) * 1408]),
            "dn_packed_sl": np.ascontiguousarray(dnp[c * DN_ROWS:(c + 1) * DN_ROWS]),
            "dn_scales_sl": np.ascontiguousarray(dns_t[c * DN_ROWS:(c + 1) * DN_ROWS]),
            "scaler_b": scaler_b, "gnw_b": gnw_b,
            "cosb": cosb, "sinb": sinb, "logpos": logp,
            "ident": ident, "identf": identf, "onesr": onesr, "mTL": mTL, "mTR": mTR,
        })
    res = run_bass_kernel_spmd(nc, in_maps, core_ids=list(range(NCORE)))
    out = np.stack([np.asarray(r["out"], dtype=np.float32) for r in res.results], axis=0)
    import os as _os
    if _os.environ.get("BASSDBG"):
        kernel._dbg = res.results
    return out
